# version 03
# baseline (speedup 1.0000x reference)
import contextlib
import numpy as np
import ml_dtypes
import concourse.bass as bass
import concourse.mybir as mybir
from concourse.bass_utils import run_bass_kernel_spmd

F32 = mybir.dt.float32
BF16 = mybir.dt.bfloat16
I32 = mybir.dt.int32
AF = mybir.ActivationFunctionType
ALU = mybir.AluOpType
AX = mybir.AxisListType.X

D = 2048
SEQ = 2048
NBC = 2
NTOK = NBC * SEQ
NT = NTOK // 128
CW = 512
NCH = NTOK // CW
CT = D // 128
NEG = -1.0e30
RT128 = 1.0 / np.sqrt(128.0)
ALIBI_SLOPES = [float((2.0 ** (-8.0 / 16.0)) ** i) for i in range(1, 17)]

ALL_SUBLAYERS = ["L0A", "L0F", "L1A", "L1F", "L2A", "L2F", "L3A", "L3F"]
SUBLAYERS = list(ALL_SUBLAYERS)
FUSED = False


def _deinterleave_cols(w, head):
    o = w.shape[-1]
    idx = []
    for h0 in range(0, o, head):
        idx.extend(range(h0, h0 + head, 2))
        idx.extend(range(h0 + 1, h0 + head, 2))
    idx = np.asarray(idx)
    return np.ascontiguousarray(w[..., idx]), idx


def _bias_cols(b):
    return np.ascontiguousarray(np.asarray(b).reshape(-1, 128).T)


def _pi_cols(n):
    idx = []
    for b0 in range(0, n, 128):
        idx.extend(range(b0, b0 + 128, 2))
        idx.extend(range(b0 + 1, b0 + 128, 2))
    return np.asarray(idx)


def prep_host(inputs):
    H = {}
    H["identf"] = np.eye(128, dtype=np.float32)
    q = np.arange(128)[:, None]
    c = np.arange(512)[None, :]
    for r in range(4):
        H[f"mask_c{r}"] = np.where(128 * r + c > q, NEG, 0.0).astype(np.float32)
    for w in range(1, 5):
        H[f"mask_w{w}"] = np.where(c >= 128 * (4 - w) + q, NEG, 0.0).astype(np.float32)
    t_ = np.arange(SEQ, dtype=np.float64)
    inv0 = 1.0 / (10000.0 ** (np.arange(0, 128, 2, dtype=np.float64) / 128.0))
    f0 = t_[:, None] * inv0[None, :]
    H["cos0"] = np.cos(f0).astype(np.float32)
    H["sin0"] = np.sin(f0).astype(np.float32)
    inv3 = 1.0 / (1000000.0 ** (np.arange(0, 256, 2, dtype=np.float64) / 256.0))
    f3 = t_[:, None] * inv3[None, :]
    emb = np.concatenate([f3, f3], axis=1)
    H["cos3"] = np.cos(emb).astype(np.float32)
    H["sin3"] = np.sin(emb).astype(np.float32)
    p_ = np.arange(128)[:, None]
    u_ = np.arange(SEQ + 896)[None, :]
    H["alT"] = (u_ - (SEQ - 1) - p_).astype(np.float32)

    g = lambda k: np.asarray(inputs[k])
    if "L0A" in SUBLAYERS:
        wq, perm = _deinterleave_cols(g("L0_attn_Wq"), 128)
        wk, permk = _deinterleave_cols(g("L0_attn_Wk"), 128)
        H["l0_wq"] = wq
        H["l0_wk"] = wk
        H["l0_wv"] = np.ascontiguousarray(g("L0_attn_Wv"))
        H["l0_wo"] = np.ascontiguousarray(g("L0_attn_Wo"))
        H["l0_bq_r"] = np.ascontiguousarray(g("L0_attn_bq")[perm].reshape(1, -1))
        H["l0_bk_r"] = np.ascontiguousarray(g("L0_attn_bk")[permk].reshape(1, -1))
        H["l0_bv_r"] = np.ascontiguousarray(g("L0_attn_bv").reshape(1, -1))
        H["l0_bo_r"] = np.ascontiguousarray(g("L0_attn_bo").reshape(1, -1))
        H["l0_alpha"] = np.ascontiguousarray(g("L0_attn_alpha").reshape(1, 16))
    if "L0F" in SUBLAYERS:
        pi_n = _pi_cols(16384)
        H["l0_pg"] = np.ascontiguousarray(g("L0_ffn_gate_W_packed").T[:, pi_n])
        H["l0_pu"] = np.ascontiguousarray(g("L0_ffn_up_W_packed").T[:, pi_n])
        H["l0_pd"] = np.ascontiguousarray(g("L0_ffn_down_W_packed").T)
        H["l0_sg"] = np.ascontiguousarray(g("L0_ffn_gate_scales").T[:, pi_n])
        H["l0_su"] = np.ascontiguousarray(g("L0_ffn_up_scales").T[:, pi_n])
        H["l0_sd"] = np.ascontiguousarray(g("L0_ffn_down_scales").T)
    if "L1A" in SUBLAYERS:
        H["l1_wq"] = np.ascontiguousarray(g("L1_attn_Wq"))
        H["l1_wk"] = np.ascontiguousarray(g("L1_attn_Wk"))
        H["l1_wv"] = np.ascontiguousarray(g("L1_attn_Wv"))
        H["l1_wo"] = np.ascontiguousarray(g("L1_attn_Wo"))
        H["l1_bq_c"] = _bias_cols(g("L1_attn_bq"))
        H["l1_bk_c"] = _bias_cols(g("L1_attn_bk"))
        H["l1_bv_r"] = np.ascontiguousarray(g("L1_attn_bv").reshape(1, -1))
        H["l1_bo_r"] = np.ascontiguousarray(g("L1_attn_bo").reshape(1, -1))
    if "L1F" in SUBLAYERS:
        H["l1f_w"] = np.ascontiguousarray(g("L1_ffn_weight").T)
        H["l1f_a"] = np.ascontiguousarray(g("L1_ffn_lora_A").transpose(2, 0, 1).reshape(D, 128))
        H["l1f_b"] = np.ascontiguousarray(g("L1_ffn_lora_B").transpose(0, 2, 1).reshape(128, D))
        H["l1f_r"] = np.ascontiguousarray(g("L1_ffn_router_w").T)
    if "L2A" in SUBLAYERS:
        H["l2_wq"] = np.ascontiguousarray(g("L2_attn_wq").reshape(D, D))
        H["l2_wk"] = np.ascontiguousarray(g("L2_attn_wk").reshape(D, D))
        H["l2_wv"] = np.ascontiguousarray(g("L2_attn_wv").reshape(D, D))
        H["l2_wo"] = np.ascontiguousarray(g("L2_attn_wo").reshape(D, D))
        H["l2_qs"] = np.ascontiguousarray(g("L2_attn_q_scale").reshape(1, 128))
        H["l2_ks"] = np.ascontiguousarray(g("L2_attn_k_scale").reshape(1, 128))
    if "L2F" in SUBLAYERS:
        H["l2f_rw"] = np.ascontiguousarray(g("L2_ffn_router_w"))
        H["l2f_rb"] = np.ascontiguousarray(g("L2_ffn_router_b").reshape(1, 8))
        H["l2f_ew"] = np.ascontiguousarray(g("L2_ffn_expert_w"))
        H["l2f_eb"] = np.ascontiguousarray(g("L2_ffn_expert_b").reshape(8, 1, D))
        H["l2f_ow"] = np.ascontiguousarray(g("L2_ffn_out_w"))
        H["l2f_ob"] = np.ascontiguousarray(g("L2_ffn_out_b").reshape(1, -1))
    if "L3A" in SUBLAYERS:
        H["l3_wq"] = np.ascontiguousarray(g("L3_attn_Wq").T)
        H["l3_wk"] = np.ascontiguousarray(g("L3_attn_Wk").T)
        H["l3_wv"] = np.ascontiguousarray(g("L3_attn_Wv").T)
        H["l3_wo"] = np.ascontiguousarray(g("L3_attn_Wo").T)
    if "L3F" in SUBLAYERS:
        H["l3f_rw"] = np.ascontiguousarray(g("L3_ffn_router_w"))
        H["l3f_a"] = np.ascontiguousarray(g("L3_ffn_Wa").transpose(1, 0, 2).reshape(D, 1024))
        H["l3f_b"] = np.ascontiguousarray(g("L3_ffn_Wb").reshape(1024, D))
    return H


class MK:
    ENG = ("t", "v", "s", "g", "y")
    EMAP = {"t": "tensor", "v": "vector", "s": "scalar", "g": "gpsimd", "y": "sync"}

    def __init__(self, H, name="mk"):
        self.H = H
        self.nc = bass.Bass("TRN2", target_bir_lowering=False)
        self.ex = contextlib.ExitStack()
        self.ops = {e: [] for e in MK.ENG}
        self.sem = {}
        for e in ("t", "v", "s", "g"):
            self.sem[e] = self.ex.enter_context(self.nc.semaphore("sem_" + e))
        self.dsems = {}
        self.cnt = {e: 0 for e in ("t", "v", "s", "g")}
        self.inputs = {}
        self.n_flush = 0
        self.uid = 0
        self.ps_mm = self.ex.enter_context(self.nc.psum_tensor("ps_mm", [128, 2048], F32))
        self.ps_a = self.ex.enter_context(self.nc.psum_tensor("ps_a", [128, 512], F32))
        self.ps_b = self.ex.enter_context(self.nc.psum_tensor("ps_b", [128, 512], F32))
        self.ps_tr = self.ex.enter_context(self.nc.psum_tensor("ps_tr", [128, 1024], BF16))
        self.ps_x = self.ex.enter_context(self.nc.psum_tensor("ps_x", [128, 512], F32))

    def inp(self, name, arr=None, dtype=None):
        if name in self.inputs:
            return self.inputs[name][1]
        if arr is None:
            arr = self.H[name]
        dt_ = dtype or {np.dtype(np.float32): F32, np.dtype(np.int32): I32}[arr.dtype]
        ap = self.nc.dram_tensor(name, list(arr.shape), dt_, kind="ExternalInput").ap()
        self.inputs[name] = (arr, ap)
        return ap

    def out(self, name, shape, dtype=F32):
        return self.nc.dram_tensor(name, list(shape), dtype, kind="ExternalOutput").ap()

    def scratch(self, name, shape, dtype):
        return self.nc.dram_tensor(name, list(shape), dtype).ap()

    def sb(self, ctx, name, shape, dtype):
        self.uid += 1
        return ctx.enter_context(self.nc.sbuf_tensor(f"{name}_{self.uid}", shape, dtype))

    def op(self, e, fn):
        if e in ("v", "s", "g") and self.cnt[e] > 0:
            psem, pv = self.sem[e], self.cnt[e]
            self.ops[e].append(lambda eng: eng.wait_ge(psem, pv))
        self.cnt[e] += 1
        sem = self.sem[e]

        def run(eng):
            r = fn(eng)
            assert r is not None, "op closure must return its instruction"
            r.then_inc(sem, 1)
        self.ops[e].append(run)
        return self.cnt[e]

    def group(self, e, fn):
        if e in ("v", "s", "g") and self.cnt[e] > 0:
            psem, pv = self.sem[e], self.cnt[e]
            self.ops[e].append(lambda eng: eng.wait_ge(psem, pv))
        self.cnt[e] += 1
        sem = self.sem[e]

        def run(eng):
            r = fn(eng)
            assert r is not None
            r.then_inc(sem, 1)
        self.ops[e].append(run)
        return self.cnt[e]

    def wait(self, e, prod, val=None):
        v = self.cnt[prod] if val is None else val
        if v is None or v <= 0:
            return
        sem = self.sem[prod]
        self.ops[e].append(lambda eng: eng.wait_ge(sem, v))

    def dma(self, out, in_, e="y", key="d", **kw):
        if key not in self.dsems:
            self.dsems[key] = [self.ex.enter_context(self.nc.semaphore("dq_" + key)), 0]
        ent = self.dsems[key]
        ent[1] += 16
        sem, tgt = ent[0], ent[1]

        def run(eng):
            eng.dma_start(out=out, in_=in_, **kw).then_inc(sem, 16)
        self.ops[e].append(run)
        return (key, tgt)

    def dwait(self, e, token):
        if not token:
            return
        key, v = token
        sem = self.dsems[key][0]
        self.ops[e].append(lambda eng: eng.wait_ge(sem, v))

    def barrier(self):
        snap = dict(self.cnt)
        dsnap = {k: ent[1] for k, ent in self.dsems.items() if ent[1] > 0}
        for e in MK.ENG:
            for p in ("t", "v", "s", "g"):
                if e != p and snap[p] > 0:
                    self.wait(e, p, snap[p])
            for k, v in dsnap.items():
                self.dwait(e, (k, v))
        if not getattr(self, "single_flush", False):
            self.flush()

    def flush(self):
        todo = {e: lst for e, lst in self.ops.items() if lst}
        if not todo:
            return
        self.n_flush += 1
        with self.nc.Block(f"blk{self.n_flush}") as b:
            for e, lst in todo.items():
                def body(eng, lst=lst):
                    for fn in lst:
                        fn(eng)
                getattr(b, MK.EMAP[e])(body)
        self.ops = {e: [] for e in MK.ENG}

    def finish(self):
        self.barrier()
        self.flush()
        self.ex.close()

    def setup_common(self, ctx):
        self.idf = self.sb(ctx, "idf", [128, 128], F32)
        self.idb = self.sb(ctx, "idb", [128, 128], BF16)
        self.ones1 = self.sb(ctx, "ones1", [1, 128], F32)
        self.onesb = self.sb(ctx, "onesb", [1, 128], BF16)
        t = self.dma(self.idf[:, :], self.inp("identf")[:, :], key="c")
        self.dwait("v", t)
        self.op("v", lambda e: e.tensor_copy(out=self.idb[:, :], in_=self.idf[:, :]))
        self.op("v", lambda e: e.memset(self.ones1[:, :], 1.0))
        self.op("v", lambda e: e.memset(self.onesb[:, :], 1.0))
        self.barrier()

    def bcast_row(self, row_ap, width, dst_sb, dst_off=0, dtype_bf=False):
        for j in range(0, width, 512):
            w = min(512, width - j)
            tw = self.group("t", lambda e, j=j, w=w: e.matmul(
                self.ps_x[:, 0:w], self.ones1[:, :], row_ap[0:1, j:j + w], start=True, stop=True))
            self.wait("v", "t", tw)
            cv = self.op("v", lambda e, j=j, w=w: e.tensor_copy(
                out=dst_sb[:, dst_off + j:dst_off + j + w], in_=self.ps_x[:, 0:w]))
            self.wait("t", "v", cv)

    def run(self, per_core_inputs):
        in_maps = []
        for ci in range(8):
            m = {n: a for n, (a, _) in self.inputs.items() if a is not None}
            m.update(per_core_inputs[ci])
            in_maps.append(m)
        return run_bass_kernel_spmd(self.nc, in_maps, core_ids=list(range(8)))


def load_weights_bf16(mk, ctx, specs):
    stage = [mk.sb(ctx, f"wstg{i}", [128, 2048], F32) for i in range(2)]
    guard = [0, 0]
    si = 0
    for ap, dst in specs:
        R, C = ap.shape
        assert C <= 2048 and R % 128 == 0
        for rt in range(R // 128):
            sl = si % 2
            ce = "v" if sl == 0 else "g"
            mk.wait("y", ce, guard[sl])
            t = mk.dma(stage[sl][:, 0:C], ap[rt * 128:(rt + 1) * 128, :], key=f"wld{sl}")
            mk.dwait(ce, t)
            guard[sl] = mk.op(ce, lambda e, sl=sl, rt=rt, C=C, dst=dst: e.tensor_copy(
                out=dst[:, rt * C:(rt + 1) * C], in_=stage[sl][:, 0:C]))
            si += 1


def load_chunk(mk, src_s, cols0, width, dst, rows=D):
    nrt = rows // 128
    src = src_s.rearrange("(ct p) t -> ct p t", p=128)[:, :, cols0:cols0 + width]
    d = dst[:, 0:nrt * width].rearrange("p (ct w) -> ct p w", w=width)
    return mk.dma(d, src)


def emit_ln_pass(mk, ctx, x_in, xnT_s, pi_order=False, tok_rln=None):
    xin = [mk.sb(ctx, f"lnx{i}", [128, 2048], F32) for i in range(2)]
    xc = [mk.sb(ctx, f"lnc{i}", [128, 2048], F32) for i in range(2)]
    xn = [mk.sb(ctx, f"lnn{i}", [128, 2048], BF16) for i in range(2)]
    sq = mk.sb(ctx, "lnsq", [128, 2048], F32)
    xnt = [mk.sb(ctx, f"lnt{i}", [128, 2048], BF16) for i in range(2)]
    col = mk.sb(ctx, "lncol", [128, 16], F32)
    last_xin = [0, 0]
    last_xc = [0, 0]
    last_xn = [0, 0]
    last_xnt = [None, None]
    ev_v = ev_s = 0
    prev_cs = 0
    for i in range(NT):
        sl = i % 2
        mk.wait("y", "g", last_xin[sl])
        tgt = mk.dma(xin[sl][:, :], x_in[i * 128:(i + 1) * 128, :], key=f"lnx{sl}")
        mk.dwait("v", tgt)
        mk.wait("v", "s", prev_cs)
        rr = mk.op("v", lambda e, sl=sl: e.reduce_sum(out=col[:, 0:1], in_=xin[sl][:, :], axis=AX))
        mk.wait("v", "v", rr)
        mv = mk.op("v", lambda e: e.tensor_scalar(out=col[:, 0:1], in0=col[:, 0:1], scalar1=1.0 / 2048.0, scalar2=None, op0=ALU.mult))
        mk.wait("g", "v", mv)
        mk.wait("g", "s", last_xc[sl])
        cg = mk.op("g", lambda e, sl=sl: e.tensor_scalar(out=xc[sl][:, :], in0=xin[sl][:, :], scalar1=col[:, 0:1], scalar2=None, op0=ALU.subtract))
        last_xin[sl] = cg
        mk.wait("v", "g", cg)
        q1 = mk.op("v", lambda e, sl=sl: e.tensor_tensor(out=sq[:, :], in0=xc[sl][:, :], in1=xc[sl][:, :], op=ALU.mult))
        mk.wait("v", "v", q1)
        q2 = mk.op("v", lambda e: e.reduce_sum(out=col[:, 1:2], in_=sq[:, :], axis=AX))
        mk.wait("v", "v", q2)
        vv = mk.op("v", lambda e: e.tensor_scalar(out=col[:, 1:2], in0=col[:, 1:2], scalar1=1.0 / 2048.0, scalar2=1e-5, op0=ALU.mult, op1=ALU.add))
        mk.wait("s", "v", vv)
        mk.op("s", lambda e: e.activation(out=col[:, 2:3], in_=col[:, 1:2], func=AF.Ln))
        sc = mk.op("s", lambda e: e.activation(out=col[:, 3:4], in_=col[:, 2:3], func=AF.Exp, scale=-0.5))
        if tok_rln is not None:
            mk.wait("v", "s", sc)
            mk.op("v", lambda e: e.tensor_scalar(out=col[:, 5:6], in0=col[:, 1:2], scalar1=1e-5, scalar2=2048.0, op0=ALU.subtract, op1=ALU.mult))
            mk.wait("s", "v")
            mk.op("s", lambda e: e.activation(out=col[:, 6:7], in_=col[:, 5:6], func=AF.Ln))
            sr = mk.op("s", lambda e: e.activation(out=col[:, 7:8], in_=col[:, 6:7], func=AF.Exp, scale=0.5))
            mk.wait("v", "s", sr)
            mk.op("v", lambda e: e.tensor_tensor(out=col[:, 8:9], in0=col[:, 7:8], in1=col[:, 3:4], op=ALU.mult))
            fv = mk.op("v", lambda e: e.tensor_scalar(out=col[:, 8:9], in0=col[:, 8:9], scalar1=1e-12, scalar2=None, op0=ALU.max))
            mk.wait("v", "v", fv)
            mk.op("v", lambda e, i=i: e.reciprocal(out=tok_rln[:, i:i + 1], in_=col[:, 8:9]))
        mk.wait("s", "s", sc)
        mk.wait("s", "t", last_xn[sl])
        cs = mk.op("s", lambda e, sl=sl: e.activation(out=xn[sl][:, :], in_=xc[sl][:, :], func=AF.Identity, scale=col[:, 3:4]))
        last_xc[sl] = cs
        prev_cs = cs
        mk.wait("t", "s", cs)
        for half in range(2):
            mk.wait("t", "v", ev_v)
            mk.wait("t", "s", ev_s)

            def tr_group(eng, sl=sl, half=half):
                r = None
                for j in range(8):
                    ct = half * 8 + j
                    if pi_order:
                        blk, off = ct // 2, ct % 2
                        src3 = xn[sl][:, :].rearrange("p (blk pair two) -> p blk pair two", blk=8, two=2)
                        src = src3[:, blk, :, off]
                    else:
                        src = xn[sl][:, ct * 128:(ct + 1) * 128]
                    r = eng.transpose(mk.ps_tr[:, j * 128:(j + 1) * 128], src, mk.idb[:, :])
                return r
            tg = mk.group("t", tr_group)
            ee = "v" if half == 0 else "s"
            mk.wait(ee, "t", tg)
            mk.dwait(ee, last_xnt[sl])
            if ee == "v":
                ev = mk.op(ee, lambda e, sl=sl, half=half: e.tensor_copy(
                    out=xnt[sl][:, half * 1024:(half + 1) * 1024], in_=mk.ps_tr[:, :]))
            else:
                ev = mk.op(ee, lambda e, sl=sl, half=half: e.copy(
                    out=xnt[sl][:, half * 1024:(half + 1) * 1024], in_=mk.ps_tr[:, :]))
            if ee == "v":
                ev_v = ev
            else:
                ev_s = ev
        last_xn[sl] = mk.cnt["t"]
        mk.wait("y", "v", ev_v)
        mk.wait("y", "s", ev_s)
        dst = xnT_s.rearrange("(ct p) t -> p ct t", p=128)[:, :, i * 128:(i + 1) * 128]
        src = xnt[sl][:, :].rearrange("p (ct c) -> p ct c", c=128)
        last_xnt[sl] = mk.dma(dst, src, key=f"lnst{sl}")
    mk.barrier()


def proj_dt(mk, ctx, xnT_s, w_sb, nout_tiles, out_s, bias_col_sb=None, post=None):
    Cout = nout_tiles * 128
    xch = [mk.sb(ctx, f"pj_x{i}", [128, CT * CW], BF16) for i in range(2)]
    ost = [mk.sb(ctx, f"pj_o{i}", [128, CW], BF16) for i in range(2)]
    ps = [mk.ps_a, mk.ps_b]
    ps_guard = [0, 0]
    ost_guard = [None, None]
    xch_guard = [0, 0]
    gi = 0
    for ch in range(NCH):
        sl = ch % 2
        mk.wait("y", "t", xch_guard[sl])
        xt = mk.dma(xch[sl][:, :].rearrange("p (ct w) -> p ct w", w=CW),
                    xnT_s.rearrange("(ct p) t -> p ct t", p=128)[:, :, ch * CW:(ch + 1) * CW], key=f"xch{sl}")
        mk.dwait("t", xt)
        for ot in range(nout_tiles):
            pi = gi % 2
            ee = "v" if gi % 2 == 0 else "s"
            mk.wait("t", "v" if pi == 0 else "s", ps_guard[pi])

            def mmg(eng, sl=sl, ot=ot, pi=pi):
                r = None
                for ct in range(CT):
                    r = eng.matmul(ps[pi][:, :], w_sb[:, ct * Cout + ot * 128:ct * Cout + ot * 128 + 128],
                                   xch[sl][:, ct * CW:(ct + 1) * CW],
                                   start=(ct == 0), stop=(ct == CT - 1))
                return r
            tg = mk.group("t", mmg)
            osl = gi % 2
            mk.wait(ee, "t", tg)
            mk.dwait(ee, ost_guard[osl])
            if post is not None:
                ev = post(mk, ee, ot, ps[pi], ost[osl])
            elif bias_col_sb is not None:
                if ee == "v":
                    ev = mk.op(ee, lambda e, ot=ot, pi=pi, osl=osl: e.tensor_scalar(
                        out=ost[osl][:, :], in0=ps[pi][:, :], scalar1=bias_col_sb[:, ot:ot + 1], scalar2=None, op0=ALU.add))
                else:
                    ev = mk.op(ee, lambda e, ot=ot, pi=pi, osl=osl: e.activation(
                        out=ost[osl][:, :], in_=ps[pi][:, :], func=AF.Identity, bias=bias_col_sb[:, ot:ot + 1]))
            else:
                if ee == "v":
                    ev = mk.op(ee, lambda e, pi=pi, osl=osl: e.tensor_copy(out=ost[osl][:, :], in_=ps[pi][:, :]))
                else:
                    ev = mk.op(ee, lambda e, pi=pi, osl=osl: e.copy(out=ost[osl][:, :], in_=ps[pi][:, :]))
            ps_guard[pi] = ev
            mk.wait("y", ee, ev)
            ost_guard[osl] = mk.dma(out_s[ot * 128:(ot + 1) * 128, ch * CW:(ch + 1) * CW], ost[osl][:, :], key=f"ost{osl}")
            gi += 1
        xch_guard[sl] = mk.cnt["t"]
    mk.barrier()


def proj_td(mk, ctx, xnT_s, w_sb, cout, out_s, bias_bc=None, out_f32_sb=None):
    noc = cout // 512
    xch = [mk.sb(ctx, f"pt_x{i}", [128, CT * CW], BF16) for i in range(2)]
    ost = [mk.sb(ctx, f"pt_o{i}", [128, cout], BF16) for i in range(2)]
    ps = [mk.ps_a, mk.ps_b]
    ps_guard = [0, 0]
    ost_guard = [None, None]
    xch_guard = [0, 0]
    gi = 0
    for ch in range(NCH):
        sl = ch % 2
        mk.wait("y", "t", xch_guard[sl])
        xt = mk.dma(xch[sl][:, :].rearrange("p (ct w) -> p ct w", w=CW),
                    xnT_s.rearrange("(ct p) t -> p ct t", p=128)[:, :, ch * CW:(ch + 1) * CW], key=f"xch{sl}")
        mk.dwait("t", xt)
        for tt in range(CW // 128):
            osl = (ch * (CW // 128) + tt) % 2
            for oc in range(noc):
                pi = gi % 2
                ee = "v"
                mk.wait("t", "v", ps_guard[pi])
                mk.dwait(ee, ost_guard[osl])

                def mmg(eng, sl=sl, tt=tt, oc=oc, pi=pi):
                    r = None
                    for ct in range(CT):
                        r = eng.matmul(ps[pi][:, :],
                                       xch[sl][:, ct * CW + tt * 128:ct * CW + tt * 128 + 128],
                                       w_sb[:, ct * cout + oc * 512:ct * cout + oc * 512 + 512],
                                       start=(ct == 0), stop=(ct == CT - 1))
                    return r
                tg = mk.group("t", mmg)
                mk.wait(ee, "t", tg)
                if bias_bc is not None:
                    ev = mk.op(ee, lambda e, pi=pi, osl=osl, oc=oc: e.tensor_tensor(
                        out=ost[osl][:, oc * 512:(oc + 1) * 512], in0=ps[pi][:, :],
                        in1=bias_bc[:, oc * 512:(oc + 1) * 512], op=ALU.add))
                else:
                    ev = mk.op(ee, lambda e, pi=pi, osl=osl, oc=oc: e.tensor_copy(
                        out=ost[osl][:, oc * 512:(oc + 1) * 512], in_=ps[pi][:, :]))
                ps_guard[pi] = ev
                gi += 1
            mk.wait("y", "v")
            mk.wait("y", "s")
            row0 = ch * CW + tt * 128
            ost_guard[osl] = mk.dma(out_s[row0:row0 + 128, 0:cout], ost[osl][:, :], key=f"ost{osl}")
    mk.barrier()


def attn_core(mk, ctx, cfg):
    HD = cfg["hd"]
    nhd = HD // 128
    NH = cfg["nheads"]
    KTILES = SEQ // 128
    QB = SEQ // 512
    ksb = mk.sb(ctx, "at_k", [128, nhd * SEQ], BF16)
    vsb = mk.sb(ctx, "at_v", [128, KTILES * HD], BF16)
    qsb = mk.sb(ctx, "at_q", [128, nhd * SEQ], BF16)
    p1 = mk.sb(ctx, "at_p1", [128, SEQ], BF16)
    s1 = mk.sb(ctx, "at_s1", [128, SEQ], F32)
    ptb = mk.sb(ctx, "at_pt", [128, KTILES * 512], BF16)
    csb = mk.sb(ctx, "at_c", [128, nhd * SEQ], BF16)
    scol = mk.sb(ctx, "at_scol", [128, 16], F32)
    rb = mk.sb(ctx, "at_rb", [128, 512], F32)
    slopes = cfg.get("alibi_slopes")
    alT = mk.sb(ctx, "at_alt", [128, SEQ + 896], F32) if slopes else None
    abc = mk.sb(ctx, "at_abc", [128, SEQ + 896], F32) if slopes else None
    if slopes:
        mk.dma(alT[:, :], mk.inp("alT")[:, :], key="c")
    maskc = [mk.inp(f"mask_c{r}") for r in range(4)]
    mcs = [mk.sb(ctx, f"at_mc{r}", [128, 512], F32) for r in range(4)]
    for r in range(4):
        mk.dma(mcs[r][:, :], maskc[r][:, :], key="c")
    win = cfg.get("window")
    if win:
        mws = [mk.sb(ctx, f"at_mw{w}", [128, 512], F32) for w in range(1, 5)]
        for w in range(1, 5):
            mk.dma(mws[w - 1][:, :], mk.inp(f"mask_w{w}")[:, :], key="c")
    mk.barrier()

    scale = cfg.get("scale", 1.0)
    subc = cfg.get("sub_const", 0.0)
    softcap = cfg.get("softcap")
    causal = cfg.get("causal", True)

    for b in range(NBC):
        for h in range(NH):
            g_ = cfg["kv_group"](h)
            tk = mk.dma(ksb[:, :].rearrange("p (n t) -> p n t", n=nhd),
                        cfg["k_s"].rearrange("(n p) t -> p n t", p=128)[:, g_ * nhd:(g_ + 1) * nhd, b * SEQ:(b + 1) * SEQ], key="atk")
            tv = mk.dma(vsb[:, :].rearrange("p (j c) -> p j c", c=HD),
                        cfg["v_s"].rearrange("(bb j p) d -> p bb j d", bb=NBC, p=128)[:, b, :, g_ * HD:(g_ + 1) * HD], key="atv")
            tq = mk.dma(qsb[:, :].rearrange("p (n t) -> p n t", n=nhd),
                        cfg["q_s"].rearrange("(n p) t -> p n t", p=128)[:, h * nhd:(h + 1) * nhd, b * SEQ:(b + 1) * SEQ], key="atq")
            if slopes:
                av = mk.op("v", lambda e, h=h: e.tensor_scalar(
                    out=abc[:, :], in0=alT[:, :], scalar1=slopes[h], scalar2=None, op0=ALU.mult))
            mk.dwait("t", tk)
            mk.dwait("t", tq)
            mk.dwait("t", tv)
            last_exp = 0
            last_stt = 0
            last_tr = 0
            ctx_mm_done = 0
            ctx_ev_done = 0
            for qb in range(QB):
                ktmax = min(KTILES, (qb + 1) * 4) if causal else KTILES
                kt0 = 0
                if win:
                    kt0 = max(0, qb * 4 - 4)
                if causal:
                    zops = list(range(qb * 4 + 1, ktmax))
                    if zops:
                        mk.wait("g", "t", ctx_mm_done)

                        def zfn(e, zops=zops):
                            r = None
                            for kt in zops:
                                r = e.memset(ptb[:, kt * 512:(kt + 1) * 512], 0.0)
                            return r
                        zg = mk.group("g", zfn)
                        mk.wait("v", "g", zg)
                mk.wait("v", "t", ctx_mm_done)
                for qi in range(qb * 4, qb * 4 + 4):
                    kmax = (qi + 1) * 128 if causal else SEQ
                    k0 = kt0 * 128
                    width = kmax - k0
                    mk.wait("t", "s", last_exp)
                    mk.wait("t", "v", last_stt)

                    def sg(eng, qi=qi, k0=k0, kmax=kmax):
                        r = None
                        for kb in range(k0 // 512, (kmax + 511) // 512):
                            wdt = min(512, kmax - kb * 512)
                            for dt_ in range(nhd):
                                r = eng.matmul(mk.ps_mm[:, kb * 512:kb * 512 + wdt],
                                               qsb[:, dt_ * SEQ + qi * 128:dt_ * SEQ + qi * 128 + 128],
                                               ksb[:, dt_ * SEQ + kb * 512:dt_ * SEQ + kb * 512 + wdt],
                                               start=(dt_ == 0), stop=(dt_ == nhd - 1))
                        return r
                    tg = mk.group("t", sg)
                    mk.wait("v", "t", tg)
                    mk.wait("v", "s", last_exp)

                    def ev(e, qi=qi, k0=k0, width=width):
                        if abc is not None:
                            off = (SEQ - 1) - 128 * qi + k0
                            return e.scalar_tensor_tensor(out=s1[:, 0:width], in0=mk.ps_mm[:, k0:k0 + width],
                                                          scalar=scale, in1=abc[:, off:off + width],
                                                          op0=ALU.mult, op1=ALU.add)
                        if subc:
                            return e.tensor_scalar(out=s1[:, 0:width], in0=mk.ps_mm[:, k0:k0 + width],
                                                   scalar1=scale, scalar2=-subc, op0=ALU.mult, op1=ALU.add)
                        return e.tensor_scalar(out=s1[:, 0:width], in0=mk.ps_mm[:, k0:k0 + width],
                                               scalar1=scale, scalar2=None, op0=ALU.mult)
                    vv = mk.op("v", ev)
                    if causal:
                        vv = mk.op("v", lambda e, qi=qi, k0=k0: e.tensor_tensor(
                            out=s1[:, qi * 128 - k0:qi * 128 - k0 + 128], in0=s1[:, qi * 128 - k0:qi * 128 - k0 + 128],
                            in1=mcs[0][:, 0:128], op=ALU.add))
                    if win and kt0 > 0:
                        def mw(e, qb=qb, kt0=kt0, k0=k0):
                            r = None
                            for kt in range(kt0, qb * 4):
                                w = qb * 4 - kt
                                off = kt * 128 - k0
                                r = e.tensor_tensor(out=s1[:, off:off + 128], in0=s1[:, off:off + 128],
                                                    in1=mws[w - 1][:, 0:128], op=ALU.add)
                            return r
                        vv = mk.op("v", mw)
                    last_stt = vv
                    mk.wait("s", "v", vv)
                    mk.wait("s", "t", last_tr)
                    if softcap:
                        sv = mk.op("s", lambda e, width=width: e.activation(
                            out=s1[:, 0:width], in_=s1[:, 0:width], func=AF.Tanh, scale=1.0 / softcap))
                        ex = mk.op("s", lambda e, qi=qi, width=width: e.activation(
                            out=p1[:, 0:width], in_=s1[:, 0:width], func=AF.Exp, scale=softcap,
                            accum_out=scol[:, qi:qi + 1]))
                    else:
                        ex = mk.op("s", lambda e, qi=qi, width=width: e.activation(
                            out=p1[:, 0:width], in_=s1[:, 0:width], func=AF.Exp,
                            accum_out=scol[:, qi:qi + 1]))
                    last_exp = ex
                    mk.wait("t", "s", ex)
                    ntr = (kmax - k0) // 128
                    evv = 0
                    for half in range(0, ntr, 8):
                        n8 = min(8, ntr - half)

                        def trg(eng, half=half, n8=n8, k0=k0):
                            r = None
                            for j in range(n8):
                                src = p1[:, (half + j) * 128:(half + j + 1) * 128]
                                r = eng.transpose(mk.ps_tr[:, j * 128:(j + 1) * 128], src, mk.idb[:, :])
                            return r
                        tg2 = mk.group("t", trg)
                        ee = "v"
                        mk.wait(ee, "t", tg2)

                        def cpg(e, half=half, n8=n8, k0=k0, qi=qi):
                            r = None
                            for j in range(n8):
                                kt = (k0 // 128) + half + j
                                r = e.tensor_copy(out=ptb[:, kt * 512 + (qi % 4) * 128:kt * 512 + (qi % 4) * 128 + 128],
                                                  in_=mk.ps_tr[:, j * 128:(j + 1) * 128])
                            return r
                        evv = mk.group(ee, cpg)
                        mk.wait("t", "v", evv)
                        last_tr = mk.cnt["t"]
                    mk.wait("v", "s", ex)
                    rv = mk.op("v", lambda e, qi=qi: e.reciprocal(out=scol[:, qi:qi + 1], in_=scol[:, qi:qi + 1]))
                    mk.wait("t", "v", rv)
                    tr2 = mk.group("t", lambda e, qi=qi: e.transpose(
                        mk.ps_x[:, (qi % 4) * 128:(qi % 4) * 128 + 128],
                        scol[:, qi:qi + 1].broadcast_to([128, 128]), mk.idf[:, :]))
                mk.wait("v", "t")
                cv2 = mk.op("v", lambda e: e.tensor_copy(out=rb[:, :], in_=mk.ps_x[:, 0:512]))
                mk.wait("t", "v", cv2)
                mk.wait("t", "v")
                for dt_ in range(nhd):
                    mk.wait("t", "v", ctx_ev_done)

                    def cg(eng, qb=qb, dt_=dt_, kt0=kt0, ktmax=ktmax):
                        r = None
                        first = True
                        for kt in range(kt0, ktmax):
                            r = eng.matmul(mk.ps_a[:, :],
                                           vsb[:, kt * HD + dt_ * 128:kt * HD + dt_ * 128 + 128],
                                           ptb[:, kt * 512:(kt + 1) * 512],
                                           start=first, stop=(kt == ktmax - 1))
                            first = False
                        return r
                    tg3 = mk.group("t", cg)
                    mk.wait("v", "t", tg3)
                    ctx_ev_done = mk.op("v", lambda e, qb=qb, dt_=dt_: e.tensor_tensor(
                        out=csb[:, dt_ * SEQ + qb * 512:dt_ * SEQ + qb * 512 + 512],
                        in0=mk.ps_a[:, :], in1=rb[:, :], op=ALU.mult))
                ctx_mm_done = mk.cnt["t"]
            mk.wait("y", "v", ctx_ev_done)
            mk.dma(cfg["ctx_s"].rearrange("(n p) t -> p n t", p=128)[:, h * nhd:(h + 1) * nhd, b * SEQ:(b + 1) * SEQ],
                   csb[:, :].rearrange("p (n t) -> p n t", n=nhd), key="atc")
            mk.barrier()
    mk.barrier()


def out_proj(mk, ctx, src_s, w_sb, x_in, x_out, bias_bc, src_rows=D):
    nct = src_rows // 128
    xch = [mk.sb(ctx, f"op_x{i}", [128, nct * CW], BF16) for i in range(2)]
    xres = [mk.sb(ctx, f"op_r{i}", [128, 2048], F32) for i in range(2)]
    ot = [mk.sb(ctx, f"op_o{i}", [128, 2048], F32) for i in range(2)]
    ps = [mk.ps_a, mk.ps_b]
    ps_guard = [0, 0]
    ot_guard = [0, 0]
    xch_guard = [0, 0]
    xres_guard_g = [0, 0]
    xres_guard_v = [0, 0]
    gi = 0
    for ch in range(NCH):
        sl = ch % 2
        mk.wait("y", "t", xch_guard[sl])
        xt = mk.dma(xch[sl][:, :].rearrange("p (ct w) -> p ct w", w=CW),
                    src_s.rearrange("(ct p) t -> p ct t", p=128)[:, :, ch * CW:(ch + 1) * CW], key=f"xch{sl}")
        mk.dwait("t", xt)
        for tt in range(CW // 128):
            ti = ch * 4 + tt
            osl = ti % 2
            mk.wait("y", "g", xres_guard_g[osl])
            mk.wait("y", "v", xres_guard_v[osl])
            rx = mk.dma(xres[osl][:, :], x_in[ti * 128:(ti + 1) * 128, :], key=f"xr{osl}")
            for oc in range(4):
                pi = gi % 2
                mk.wait("t", "v", ps_guard[pi])

                def mmg(eng, sl=sl, tt=tt, oc=oc, pi=pi):
                    r = None
                    for ct in range(nct):
                        r = eng.matmul(ps[pi][:, :],
                                       xch[sl][:, ct * CW + tt * 128:ct * CW + tt * 128 + 128],
                                       w_sb[:, ct * 2048 + oc * 512:ct * 2048 + oc * 512 + 512],
                                       start=(ct == 0), stop=(ct == nct - 1))
                    return r
                tg = mk.group("t", mmg)
                mk.wait("v", "t", tg)
                mk.dwait("v", ot_guard[osl])
                if bias_bc is not None:
                    bv = mk.op("v", lambda e, pi=pi, osl=osl, oc=oc: e.tensor_tensor(
                        out=ot[osl][:, oc * 512:(oc + 1) * 512], in0=ps[pi][:, :],
                        in1=bias_bc[:, oc * 512:(oc + 1) * 512], op=ALU.add))
                    ps_guard[pi] = bv
                    mk.wait("g", "v", bv)
                    mk.dwait("g", rx)
                    mk.op("g", lambda e, pi=pi, osl=osl, oc=oc: e.tensor_tensor(
                        out=ot[osl][:, oc * 512:(oc + 1) * 512], in0=ot[osl][:, oc * 512:(oc + 1) * 512],
                        in1=xres[osl][:, oc * 512:(oc + 1) * 512], op=ALU.add))
                else:
                    mk.dwait("v", rx)
                    bv = mk.op("v", lambda e, pi=pi, osl=osl, oc=oc: e.tensor_tensor(
                        out=ot[osl][:, oc * 512:(oc + 1) * 512], in0=ps[pi][:, :],
                        in1=xres[osl][:, oc * 512:(oc + 1) * 512], op=ALU.add))
                    ps_guard[pi] = bv
                gi += 1
            if bias_bc is not None:
                mk.wait("y", "g")
            else:
                mk.wait("y", "v")
            ot_guard[osl] = mk.dma(x_out[ti * 128:(ti + 1) * 128, :], ot[osl][:, :], key=f"xo{osl}")
            xres_guard_g[osl] = mk.cnt["g"]
            xres_guard_v[osl] = mk.cnt["v"]
    mk.barrier()


def qk_pass(mk, ctx, xnT_s, w_sb, cout, out_s, post, bias_bc=None, transpose_out=True):
    noc = cout // 512
    nbl = cout // 128
    xch = [mk.sb(ctx, f"qk_x{i}", [128, CT * CW], BF16) for i in range(2)]
    work = [mk.sb(ctx, f"qk_w{i}", [128, cout], F32) for i in range(2)]
    outb = [mk.sb(ctx, f"qk_o{i}", [128, cout], BF16) for i in range(2)]
    txb = [mk.sb(ctx, f"qk_t{i}", [128, cout], BF16) for i in range(2)]
    ps = [mk.ps_a, mk.ps_b]
    ps_guard = [0, 0]
    st_guard = [None, None]
    xch_guard = [0, 0]
    work_guard = [{}, {}]
    outb_guard = [0, 0]
    gi = 0
    for ch in range(NCH):
        sl = ch % 2
        mk.wait("y", "t", xch_guard[sl])
        xt = mk.dma(xch[sl][:, :].rearrange("p (ct w) -> p ct w", w=CW),
                    xnT_s.rearrange("(ct p) t -> p ct t", p=128)[:, :, ch * CW:(ch + 1) * CW], key=f"xch{sl}")
        mk.dwait("t", xt)
        for tt in range(CW // 128):
            ti = ch * 4 + tt
            osl = ti % 2
            for eng_, c_ in work_guard[osl].items():
                mk.wait("v", eng_, c_)
            for oc in range(noc):
                pi = gi % 2
                mk.wait("t", "v", ps_guard[pi])

                def mmg(eng, sl=sl, tt=tt, oc=oc, pi=pi):
                    r = None
                    for ct in range(CT):
                        r = eng.matmul(ps[pi][:, :],
                                       xch[sl][:, ct * CW + tt * 128:ct * CW + tt * 128 + 128],
                                       w_sb[:, ct * cout + oc * 512:ct * cout + oc * 512 + 512],
                                       start=(ct == 0), stop=(ct == CT - 1))
                    return r
                tg = mk.group("t", mmg)
                mk.wait("v", "t", tg)
                if bias_bc is not None:
                    ev = mk.op("v", lambda e, pi=pi, osl=osl, oc=oc: e.tensor_tensor(
                        out=work[osl][:, oc * 512:(oc + 1) * 512], in0=ps[pi][:, :],
                        in1=bias_bc[:, oc * 512:(oc + 1) * 512], op=ALU.add))
                else:
                    ev = mk.op("v", lambda e, pi=pi, osl=osl, oc=oc: e.tensor_copy(
                        out=work[osl][:, oc * 512:(oc + 1) * 512], in_=ps[pi][:, :]))
                ps_guard[pi] = ev
                gi += 1
            mk.wait("v", "t", outb_guard[osl])
            if not transpose_out:
                mk.dwait("v", st_guard[osl])
            pg = post(mk, ti, work[osl], outb[osl])
            work_guard[osl] = pg
            if transpose_out:
                for half in range(0, nbl, 8):
                    n8 = min(8, nbl - half)
                    for eng_, c_ in pg.items():
                        mk.wait("t", eng_, c_)
                    mk.wait("t", "v")

                    def trg(eng, half=half, n8=n8, osl=osl):
                        r = None
                        for j in range(n8):
                            r = eng.transpose(mk.ps_tr[:, j * 128:(j + 1) * 128],
                                              outb[osl][:, (half + j) * 128:(half + j + 1) * 128], mk.idb[:, :])
                        return r
                    tg2 = mk.group("t", trg)
                    mk.wait("v", "t", tg2)
                    mk.dwait("v", st_guard[osl])
                    cv = mk.op("v", lambda e, half=half, n8=n8, osl=osl: e.tensor_copy(
                        out=txb[osl][:, half * 128:(half + n8) * 128], in_=mk.ps_tr[:, 0:n8 * 128]))
                    mk.wait("t", "v", cv)
                mk.wait("y", "v", cv)
                outb_guard[osl] = mk.cnt["t"]
                dst = out_s.rearrange("(bl p) t -> p bl t", p=128)[:, 0:nbl, ti * 128:(ti + 1) * 128]
                st_guard[osl] = mk.dma(dst, txb[osl][:, :].rearrange("p (bl c) -> p bl c", c=128), key=f"qst{osl}")
            else:
                for eng_, c_ in pg.items():
                    mk.wait("y", eng_, c_)
                st_guard[osl] = mk.dma(out_s[ti * 128:(ti + 1) * 128, 0:cout], outb[osl][:, :], key=f"qst{osl}")
                outb_guard[osl] = 0
    mk.barrier()


def _norm_cols_sqrt(mk, colbuf, src_c, dst_c, scale=1.0, eps=0.0, n=16):
    if eps:
        ev = mk.op("v", lambda e: e.tensor_scalar(out=colbuf[:, dst_c:dst_c + n], in0=colbuf[:, src_c:src_c + n],
                                                  scalar1=scale, scalar2=eps, op0=ALU.mult, op1=ALU.add))
        mk.wait("s", "v", ev)
        mk.op("s", lambda e: e.activation(out=colbuf[:, dst_c:dst_c + n], in_=colbuf[:, dst_c:dst_c + n], func=AF.Ln))
    else:
        mk.wait("s", "v")
        mk.op("s", lambda e: e.activation(out=colbuf[:, dst_c:dst_c + n], in_=colbuf[:, src_c:src_c + n],
                                          func=AF.Ln, scale=scale))
    return mk.op("s", lambda e: e.activation(out=colbuf[:, dst_c:dst_c + n], in_=colbuf[:, dst_c:dst_c + n],
                                             func=AF.Exp, scale=0.5))


def emit_L0A(mk, x_in, x_out, sc):
    with contextlib.ExitStack() as ctx:
        emit_ln_pass(mk, ctx, x_in, sc["xnT"])
    with contextlib.ExitStack() as ctx:
        cosb = mk.sb(ctx, "l0_cos", [128, 1024], F32)
        sinb = mk.sb(ctx, "l0_sin", [128, 1024], F32)
        mk.dma(cosb[:, :].rearrange("p (bl c) -> p bl c", c=64),
               mk.inp("cos0").rearrange("(bl p) c -> p bl c", p=128), key="c")
        mk.dma(sinb[:, :].rearrange("p (bl c) -> p bl c", c=64),
               mk.inp("sin0").rearrange("(bl p) c -> p bl c", p=128), key="c")
        alr = mk.sb(ctx, "l0_alr", [1, 16], F32)
        alb = mk.sb(ctx, "l0_alb", [128, 16], F32)
        colq = mk.sb(ctx, "l0_colq", [128, 64], F32)
        w2 = mk.sb(ctx, "l0_w2", [128, 2048], F32)
        bqr = mk.sb(ctx, "l0_bqr", [1, 2048], F32)
        bqb = mk.sb(ctx, "l0_bqb", [128, 2048], F32)
        t = mk.dma(alr[:, :], mk.inp("l0_alpha")[:, :], key="c")
        t2 = mk.dma(bqr[:, :], mk.inp("l0_bq_r")[:, :], key="c")
        mk.dwait("t", t2)
        mk.dwait("v", t2)
        mk.bcast_row(alr, 16, alb)
        mk.op("v", lambda e: e.tensor_scalar(out=alb[:, :], in0=alb[:, :], scalar1=float(np.sqrt(128.0)), scalar2=None, op0=ALU.mult))
        mk.bcast_row(bqr, 2048, bqb)
        mk.barrier()

        def post_norm_rope(nh, alpha, eps_out):
            def post(mk, ti, work, outb):
                blk = ti % 16
                cosv = cosb[:, blk * 64:(blk + 1) * 64].unsqueeze(1).broadcast_to([128, nh, 64])
                sinv = sinb[:, blk * 64:(blk + 1) * 64].unsqueeze(1).broadcast_to([128, nh, 64])
                w3 = work[:, :].rearrange("p (h c) -> p h c", h=nh)
                q1 = w3[:, :, 0:64]
                q2 = w3[:, :, 64:128]
                sq = mk.op("v", lambda e: e.tensor_tensor(out=w2[:, 0:nh * 128], in0=work[:, :], in1=work[:, :], op=ALU.mult))
                mk.wait("v", "v", sq)
                rd = mk.op("v", lambda e: e.reduce_sum(
                    out=colq[:, 0:nh].unsqueeze(2),
                    in_=w2[:, 0:nh * 128].rearrange("p (h c) -> p h c", h=nh), axis=AX))
                mk.wait("v", "v", rd)
                nr = _norm_cols_sqrt(mk, colq, 0, 16, n=nh)
                mk.wait("v", "s", nr)
                mk.op("v", lambda e: e.tensor_scalar(out=colq[:, 16:16 + nh], in0=colq[:, 16:16 + nh], scalar1=1e-6, scalar2=None, op0=ALU.add))
                mk.wait("v", "v")
                rc = mk.op("v", lambda e: e.reciprocal(out=colq[:, 32:32 + nh], in_=colq[:, 16:16 + nh]))
                if alpha:
                    mk.wait("v", "v", rc)
                    rc = mk.op("v", lambda e: e.tensor_tensor(out=colq[:, 32:32 + nh], in0=colq[:, 32:32 + nh], in1=alb[:, 0:nh], op=ALU.mult))
                mk.wait("v", "v", rc)
                fv = colq[:, 32:32 + nh].unsqueeze(2).broadcast_to([128, nh, 128])
                sc_ = mk.op("v", lambda e: e.tensor_tensor(
                    out=w2[:, 0:nh * 128].rearrange("p (h c) -> p h c", h=nh), in0=w3, in1=fv, op=ALU.mult))
                mk.wait("v", "v", sc_)
                s3 = w2[:, 0:nh * 128].rearrange("p (h c) -> p h c", h=nh)
                s1_, s2_ = s3[:, :, 0:64], s3[:, :, 64:128]
                o3 = outb[:, :].rearrange("p (h c) -> p h c", h=nh)
                mk.op("v", lambda e: e.tensor_tensor(out=w3[:, :, 0:64], in0=s1_, in1=cosv, op=ALU.mult))
                mk.wait("v", "v")
                mk.op("v", lambda e: e.tensor_tensor(out=w3[:, :, 64:128], in0=s2_, in1=sinv, op=ALU.mult))
                mk.wait("v", "v")
                mk.op("v", lambda e: e.tensor_tensor(out=o3[:, :, 0:64], in0=w3[:, :, 0:64], in1=w3[:, :, 64:128], op=ALU.subtract))
                mk.wait("v", "v")
                mk.op("v", lambda e: e.tensor_tensor(out=w3[:, :, 0:64], in0=s1_, in1=sinv, op=ALU.mult))
                mk.wait("v", "v")
                mk.op("v", lambda e: e.tensor_tensor(out=w3[:, :, 64:128], in0=s2_, in1=cosv, op=ALU.mult))
                mk.wait("v", "v")
                lv = mk.op("v", lambda e: e.tensor_tensor(out=o3[:, :, 64:128], in0=w3[:, :, 0:64], in1=w3[:, :, 64:128], op=ALU.add))
                return {"v": lv}
            return post

        wq_sb = mk.sb(ctx, "l0_wq", [128, CT * 2048], BF16)
        with contextlib.ExitStack() as c2:
            load_weights_bf16(mk, c2, [(mk.inp("l0_wq"), wq_sb)])
            mk.barrier()
        with contextlib.ExitStack() as c2:
            qk_pass(mk, c2, sc["xnT"], wq_sb, 2048, sc["q"], post_norm_rope(16, True, None), bias_bc=bqb)
        t3 = mk.dma(bqr[:, 0:512], mk.inp("l0_bk_r")[:, :], key="c")
        mk.dwait("t", t3)
        mk.dwait("v", t3)
        mk.bcast_row(bqr, 512, bqb)
        mk.barrier()
        wk_sb = wq_sb[:, 0:CT * 512]
        with contextlib.ExitStack() as c2:
            load_weights_bf16(mk, c2, [(mk.inp("l0_wk"), wk_sb)])
            mk.barrier()
        with contextlib.ExitStack() as c2:
            qk_pass(mk, c2, sc["xnT"], wk_sb, 512, sc["k"], post_norm_rope(4, False, None), bias_bc=bqb)
        t4 = mk.dma(bqr[:, 0:512], mk.inp("l0_bv_r")[:, :], key="c")
        mk.dwait("t", t4)
        mk.dwait("v", t4)
        mk.bcast_row(bqr, 512, bqb)
        mk.barrier()
        with contextlib.ExitStack() as c2:
            load_weights_bf16(mk, c2, [(mk.inp("l0_wv"), wk_sb)])
            mk.barrier()
        with contextlib.ExitStack() as c2:
            proj_td(mk, c2, sc["xnT"], wk_sb, 512, sc["v"], bias_bc=bqb)
    with contextlib.ExitStack() as ctx:
        attn_core(mk, ctx, dict(
            nheads=16, hd=128, kv_group=lambda h: h // 4, n_kv=4,
            q_s=sc["q"], k_s=sc["k"], v_s=sc["v"], ctx_s=sc["c"],
            causal=True, scale=RT128))
    with contextlib.ExitStack() as ctx:
        wo_sb = mk.sb(ctx, "l0_wo", [128, CT * 2048], BF16)
        bor = mk.sb(ctx, "l0_bor", [1, 2048], F32)
        bob = mk.sb(ctx, "l0_bob", [128, 2048], F32)
        t = mk.dma(bor[:, :], mk.inp("l0_bo_r")[:, :], key="c")
        mk.dwait("t", t)
        mk.dwait("v", t)
        mk.bcast_row(bor, 2048, bob)
        load_weights_bf16(mk, ctx, [(mk.inp("l0_wo"), wo_sb)])
        mk.barrier()
        out_proj(mk, ctx, sc["c"], wo_sb, x_in, x_out, bob)


def emit_L2A(mk, x_in, x_out, sc):
    with contextlib.ExitStack() as ctx:
        emit_ln_pass(mk, ctx, x_in, sc["xnT"])
    with contextlib.ExitStack() as ctx:
        colq = mk.sb(ctx, "l2_colq", [128, 64], F32)
        w2 = mk.sb(ctx, "l2_w2", [128, 2048], F32)
        scr = mk.sb(ctx, "l2_scr", [1, 128], F32)
        scb = mk.sb(ctx, "l2_scb", [128, 128], F32)

        def post_rms(scale_bc):
            def post(mk, ti, work, outb):
                w3 = work[:, :].rearrange("p (h c) -> p h c", h=16)
                sq = mk.op("v", lambda e: e.tensor_tensor(out=w2[:, :], in0=work[:, :], in1=work[:, :], op=ALU.mult))
                mk.wait("v", "v", sq)
                rd = mk.op("v", lambda e: e.reduce_sum(
                    out=colq[:, 0:16].unsqueeze(2), in_=w2[:, :].rearrange("p (h c) -> p h c", h=16), axis=AX))
                mk.wait("v", "v", rd)
                e1 = mk.op("v", lambda e: e.tensor_scalar(out=colq[:, 16:32], in0=colq[:, 0:16], scalar1=1.0 / 128.0, scalar2=1e-6, op0=ALU.mult, op1=ALU.add))
                mk.wait("s", "v", e1)
                mk.op("s", lambda e: e.activation(out=colq[:, 16:32], in_=colq[:, 16:32], func=AF.Ln))
                rs = mk.op("s", lambda e: e.activation(out=colq[:, 16:32], in_=colq[:, 16:32], func=AF.Exp, scale=-0.5))
                mk.wait("v", "s", rs)
                fv = colq[:, 16:32].unsqueeze(2).broadcast_to([128, 16, 128])
                mk.op("v", lambda e: e.tensor_tensor(out=w3, in0=w3, in1=fv, op=ALU.mult))
                mk.wait("v", "v")
                scv = scale_bc[:, :].unsqueeze(1).broadcast_to([128, 16, 128])
                o3 = outb[:, :].rearrange("p (h c) -> p h c", h=16)
                lv = mk.op("v", lambda e: e.tensor_tensor(out=o3, in0=w3, in1=scv, op=ALU.mult))
                return {"v": lv}
            return post

        wq_sb = mk.sb(ctx, "l2_wq", [128, CT * 2048], BF16)
        t = mk.dma(scr[:, :], mk.inp("l2_qs")[:, :], key="c")
        mk.dwait("t", t)
        mk.dwait("v", t)
        mk.bcast_row(scr, 128, scb)
        mk.barrier()
        with contextlib.ExitStack() as c2:
            load_weights_bf16(mk, c2, [(mk.inp("l2_wq"), wq_sb)])
            mk.barrier()
        with contextlib.ExitStack() as c2:
            qk_pass(mk, c2, sc["xnT"], wq_sb, 2048, sc["q"], post_rms(scb))
        t = mk.dma(scr[:, :], mk.inp("l2_ks")[:, :], key="c")
        mk.dwait("t", t)
        mk.dwait("v", t)
        mk.bcast_row(scr, 128, scb)
        mk.barrier()
        with contextlib.ExitStack() as c2:
            load_weights_bf16(mk, c2, [(mk.inp("l2_wk"), wq_sb)])
            mk.barrier()
        with contextlib.ExitStack() as c2:
            qk_pass(mk, c2, sc["xnT"], wq_sb, 2048, sc["k"], post_rms(scb))
        with contextlib.ExitStack() as c2:
            load_weights_bf16(mk, c2, [(mk.inp("l2_wv"), wq_sb)])
            mk.barrier()
        with contextlib.ExitStack() as c2:
            proj_td(mk, c2, sc["xnT"], wq_sb, 2048, sc["v"], bias_bc=None)
    with contextlib.ExitStack() as ctx:
        attn_core(mk, ctx, dict(
            nheads=16, hd=128, kv_group=lambda h: h, n_kv=16,
            q_s=sc["q"], k_s=sc["k"], v_s=sc["v"], ctx_s=sc["c"],
            causal=False, scale=1.0, sub_const=60.0))
    with contextlib.ExitStack() as ctx:
        wo_sb = mk.sb(ctx, "l2_wo", [128, CT * 2048], BF16)
        load_weights_bf16(mk, ctx, [(mk.inp("l2_wo"), wo_sb)])
        mk.barrier()
        out_proj(mk, ctx, sc["c"], wo_sb, x_in, x_out, None)


def emit_L3A(mk, x_in, x_out, sc):
    with contextlib.ExitStack() as ctx:
        emit_ln_pass(mk, ctx, x_in, sc["xnT"])
    with contextlib.ExitStack() as ctx:
        cosb = mk.sb(ctx, "l3_cos", [128, 2048], F32)
        sinb = mk.sb(ctx, "l3_sin", [128, 2048], F32)
        mk.dma(cosb[:, :].rearrange("p (bl c) -> p bl c", c=128),
               mk.inp("cos3")[:, 0:128].rearrange("(bl p) c -> p bl c", p=128), key="c")
        mk.dma(sinb[:, :].rearrange("p (bl c) -> p bl c", c=128),
               mk.inp("sin3")[:, 0:128].rearrange("(bl p) c -> p bl c", p=128), key="c")
        colq = mk.sb(ctx, "l3_colq", [128, 64], F32)
        w2 = mk.sb(ctx, "l3_w2", [128, 2048], F32)
        mk.barrier()

        def post_rope_ln(nh, rope, outscale):
            def post(mk, ti, work, outb):
                w3 = work[:, :].rearrange("p (h c) -> p h c", h=nh)
                if rope:
                    blk = ti % 16
                    cosv = cosb[:, blk * 128:(blk + 1) * 128].unsqueeze(1).broadcast_to([128, nh, 128])
                    sinv = sinb[:, blk * 128:(blk + 1) * 128].unsqueeze(1).broadcast_to([128, nh, 128])
                    q1, q2 = w3[:, :, 0:128], w3[:, :, 128:256]
                    w23 = w2[:, 0:nh * 256].rearrange("p (h c) -> p h c", h=nh)
                    mk.op("v", lambda e: e.tensor_tensor(out=w23[:, :, 0:128], in0=q1, in1=cosv, op=ALU.mult))
                    mk.wait("v", "v")
                    mk.op("v", lambda e: e.tensor_tensor(out=w23[:, :, 128:256], in0=q2, in1=sinv, op=ALU.mult))
                    mk.wait("v", "v")
                    r1 = mk.op("v", lambda e: e.tensor_tensor(out=w23[:, :, 0:128], in0=w23[:, :, 0:128], in1=w23[:, :, 128:256], op=ALU.subtract))
                    mk.wait("v", "v")
                    mk.op("v", lambda e: e.tensor_tensor(out=w23[:, :, 128:256], in0=q2, in1=cosv, op=ALU.mult))
                    mk.wait("v", "v")
                    mk.op("v", lambda e: e.tensor_tensor(out=w3[:, :, 0:128], in0=q1, in1=sinv, op=ALU.mult))
                    mk.wait("v", "v")
                    mk.op("v", lambda e: e.tensor_tensor(out=w3[:, :, 128:256], in0=w3[:, :, 0:128], in1=w23[:, :, 128:256], op=ALU.add))
                    mk.wait("v", "v")
                    mk.op("v", lambda e: e.tensor_copy(out=w3[:, :, 0:128], in_=w23[:, :, 0:128]))
                    mk.wait("v", "v")
                rd = mk.op("v", lambda e: e.reduce_sum(out=colq[:, 0:nh].unsqueeze(2), in_=w3, axis=AX))
                mk.wait("v", "v", rd)
                mk.op("v", lambda e: e.tensor_scalar(out=colq[:, 0:nh], in0=colq[:, 0:nh], scalar1=1.0 / 256.0, scalar2=None, op0=ALU.mult))
                mk.wait("v", "v")
                mv = colq[:, 0:nh].unsqueeze(2).broadcast_to([128, nh, 256])
                xc = mk.op("v", lambda e: e.tensor_tensor(out=w3, in0=w3, in1=mv, op=ALU.subtract))
                mk.wait("v", "v", xc)
                mk.op("v", lambda e: e.tensor_tensor(out=w2[:, 0:nh * 256], in0=work[:, 0:nh * 256], in1=work[:, 0:nh * 256], op=ALU.mult))
                mk.wait("v", "v")
                rd2 = mk.op("v", lambda e: e.reduce_sum(
                    out=colq[:, 16:16 + nh].unsqueeze(2), in_=w2[:, 0:nh * 256].rearrange("p (h c) -> p h c", h=nh), axis=AX))
                mk.wait("v", "v", rd2)
                e2 = mk.op("v", lambda e: e.tensor_scalar(out=colq[:, 32:32 + nh], in0=colq[:, 16:16 + nh], scalar1=1.0 / 256.0, scalar2=1e-6, op0=ALU.mult, op1=ALU.add))
                mk.wait("s", "v", e2)
                mk.op("s", lambda e: e.activation(out=colq[:, 32:32 + nh], in_=colq[:, 32:32 + nh], func=AF.Ln))
                rs = mk.op("s", lambda e: e.activation(out=colq[:, 32:32 + nh], in_=colq[:, 32:32 + nh], func=AF.Exp, scale=-0.5))
                if outscale != 1.0:
                    mk.wait("v", "s", rs)
                    rs = mk.op("v", lambda e: e.tensor_scalar(out=colq[:, 32:32 + nh], in0=colq[:, 32:32 + nh], scalar1=outscale, scalar2=None, op0=ALU.mult))
                    mk.wait("v", "v", rs)
                else:
                    mk.wait("v", "s", rs)
                rv = colq[:, 32:32 + nh].unsqueeze(2).broadcast_to([128, nh, 256])
                o3 = outb[:, 0:nh * 256].rearrange("p (h c) -> p h c", h=nh)
                lv = mk.op("v", lambda e: e.tensor_tensor(out=o3, in0=w3, in1=rv, op=ALU.mult))
                return {"v": lv}
            return post

        wq_sb = mk.sb(ctx, "l3_wq", [128, CT * 2048], BF16)
        with contextlib.ExitStack() as c2:
            load_weights_bf16(mk, c2, [(mk.inp("l3_wq"), wq_sb)])
            mk.barrier()
        with contextlib.ExitStack() as c2:
            qk_pass(mk, c2, sc["xnT"], wq_sb, 2048, sc["q"], post_rope_ln(8, True, 256.0))
        wk_sb = wq_sb[:, 0:CT * 512]
        with contextlib.ExitStack() as c2:
            load_weights_bf16(mk, c2, [(mk.inp("l3_wk"), wk_sb)])
            mk.barrier()
        with contextlib.ExitStack() as c2:
            qk_pass(mk, c2, sc["xnT"], wk_sb, 512, sc["k"], post_rope_ln(2, True, 1.0))
        with contextlib.ExitStack() as c2:
            load_weights_bf16(mk, c2, [(mk.inp("l3_wv"), wk_sb)])
            mk.barrier()
        with contextlib.ExitStack() as c2:
            qk_pass(mk, c2, sc["xnT"], wk_sb, 512, sc["v"], post_rope_ln(2, False, 1.0), transpose_out=False)
    with contextlib.ExitStack() as ctx:
        attn_core(mk, ctx, dict(
            nheads=8, hd=256, kv_group=lambda h: h // 4, n_kv=2,
            q_s=sc["q"], k_s=sc["k"], v_s=sc["v"], ctx_s=sc["c"],
            causal=True, window=512, scale=1.0 / 16.0, softcap=30.0))
    with contextlib.ExitStack() as ctx:
        wo_sb = mk.sb(ctx, "l3_wo", [128, CT * 2048], BF16)
        load_weights_bf16(mk, ctx, [(mk.inp("l3_wo"), wo_sb)])
        mk.barrier()
        out_proj(mk, ctx, sc["c"], wo_sb, x_in, x_out, None)


def _router_top2(mk, colb, logits_psum, g_out, renorm_extra=1.0, bias_bc=None):
    if bias_bc is not None:
        lv = mk.op("v", lambda e: e.tensor_tensor(out=g_out[:, :], in0=logits_psum, in1=bias_bc[:, 0:8], op=ALU.add))
    else:
        lv = mk.op("v", lambda e: e.tensor_copy(out=g_out[:, :], in_=logits_psum))
    mk.wait("s", "v", lv)
    ev = mk.op("s", lambda e: e.activation(out=g_out[:, :], in_=g_out[:, :], func=AF.Exp))
    mk.wait("v", "s", ev)
    m1 = mk.op("v", lambda e: e.reduce_max(out=colb[:, 0:1], in_=g_out[:, :], axis=AX))
    mk.wait("v", "v", m1)
    ms = mk.op("v", lambda e: e.tensor_scalar(out=colb[:, 8:16], in0=g_out[:, :], scalar1=colb[:, 0:1], scalar2=None, op0=ALU.is_lt))
    mk.wait("v", "v", ms)
    mk.op("v", lambda e: e.tensor_tensor(out=colb[:, 8:16], in0=colb[:, 8:16], in1=g_out[:, :], op=ALU.mult))
    mk.wait("v", "v")
    m2 = mk.op("v", lambda e: e.reduce_max(out=colb[:, 1:2], in_=colb[:, 8:16], axis=AX))
    mk.wait("v", "v", m2)
    mk.op("v", lambda e: e.tensor_tensor(out=colb[:, 2:3], in0=colb[:, 0:1], in1=colb[:, 1:2], op=ALU.add))
    mk.wait("v", "v")
    mk.op("v", lambda e: e.reciprocal(out=colb[:, 2:3], in_=colb[:, 2:3]))
    if renorm_extra != 1.0:
        mk.wait("v", "v")
        mk.op("v", lambda e: e.tensor_scalar(out=colb[:, 2:3], in0=colb[:, 2:3], scalar1=renorm_extra, scalar2=None, op0=ALU.mult))
    mk.wait("v", "v")
    kp = mk.op("v", lambda e: e.tensor_scalar(out=colb[:, 8:16], in0=g_out[:, :], scalar1=colb[:, 1:2], scalar2=None, op0=ALU.is_ge))
    mk.wait("v", "v", kp)
    mk.op("v", lambda e: e.tensor_tensor(out=g_out[:, :], in0=g_out[:, :], in1=colb[:, 8:16], op=ALU.mult))
    mk.wait("v", "v")
    gv = mk.op("v", lambda e: e.tensor_scalar(out=g_out[:, :], in0=g_out[:, :], scalar1=colb[:, 2:3], scalar2=None, op0=ALU.mult))
    return gv


def emit_L1F(mk, x_in, x_out, sc):
    with contextlib.ExitStack() as ctx:
        emit_ln_pass(mk, ctx, x_in, sc["xnT"])
    with contextlib.ExitStack() as ctx:
        w_sb = mk.sb(ctx, "f1_w", [128, CT * 2048], BF16)
        a_sb = mk.sb(ctx, "f1_a", [128, CT * 128], BF16)
        b_sb = mk.sb(ctx, "f1_b", [128, 2048], BF16)
        r_sb = mk.sb(ctx, "f1_r", [128, CT * 8], BF16)
        with contextlib.ExitStack() as c2:
            load_weights_bf16(mk, c2, [(mk.inp("l1f_w"), w_sb), (mk.inp("l1f_a"), a_sb),
                                       (mk.inp("l1f_b"), b_sb), (mk.inp("l1f_r"), r_sb)])
            mk.barrier()
        xch = [mk.sb(ctx, f"f1_x{i}", [128, CT * CW], BF16) for i in range(2)]
        xres = [mk.sb(ctx, f"f1_rs{i}", [128, 2048], F32) for i in range(2)]
        ot = [mk.sb(ctx, f"f1_o{i}", [128, 2048], F32) for i in range(2)]
        colb = mk.sb(ctx, "f1_col", [128, 16], F32)
        gbuf = mk.sb(ctx, "f1_g", [128, 8], F32)
        axg = mk.sb(ctx, "f1_axg", [128, 128], BF16)
        axgT = [mk.sb(ctx, f"f1_axT{i}", [128, 128], BF16) for i in range(2)]
        ps = [mk.ps_a, mk.ps_b]
        ps_guard = [0, 0]
        ot_guard = [None, None]
        xch_guard = [0, 0]
        xres_guard_v = [0, 0]
        axgT_guard = [0, 0]
        gi = 0
        for ch in range(NCH):
            sl = ch % 2
            mk.wait("y", "t", xch_guard[sl])
            xt = mk.dma(xch[sl][:, :].rearrange("p (ct w) -> p ct w", w=CW),
                        sc["xnT"].rearrange("(ct p) t -> p ct t", p=128)[:, :, ch * CW:(ch + 1) * CW], key=f"xch{sl}")
            mk.dwait("t", xt)
            for tt in range(CW // 128):
                ti = ch * 4 + tt
                osl = ti % 2
                mk.wait("y", "v", xres_guard_v[osl])
                rx = mk.dma(xres[osl][:, :], x_in[ti * 128:(ti + 1) * 128, :], key=f"xr{osl}")
                mk.wait("t", "v")

                def lg(eng, sl=sl, tt=tt):
                    r = None
                    for ct in range(CT):
                        r = eng.matmul(mk.ps_x[:, 0:8],
                                       xch[sl][:, ct * CW + tt * 128:ct * CW + tt * 128 + 128],
                                       r_sb[:, ct * 8:(ct + 1) * 8],
                                       start=(ct == 0), stop=(ct == CT - 1))
                    for ct in range(CT):
                        r = eng.matmul(mk.ps_mm[:, 0:128],
                                       xch[sl][:, ct * CW + tt * 128:ct * CW + tt * 128 + 128],
                                       a_sb[:, ct * 128:(ct + 1) * 128],
                                       start=(ct == 0), stop=(ct == CT - 1))
                    return r
                tg = mk.group("t", lg)
                mk.wait("v", "t", tg)
                _router_top2(mk, colb, mk.ps_x[:, 0:8], gbuf, renorm_extra=2.0)
                mk.wait("v", "v")
                gx = gbuf[:, :].unsqueeze(2).broadcast_to([128, 8, 16])
                ax = mk.op("v", lambda e: e.tensor_tensor(
                    out=axg[:, :].rearrange("p (e r) -> p e r", e=8), in0=mk.ps_mm[:, 0:128].rearrange("p (e r) -> p e r", e=8),
                    in1=gx, op=ALU.mult))
                mk.wait("t", "v", ax)
                tr = mk.group("t", lambda e: e.transpose(mk.ps_tr[:, 0:128], axg[:, :], mk.idb[:, :]))
                mk.wait("v", "t", tr)
                mk.wait("v", "t", axgT_guard[osl])
                cv = mk.op("v", lambda e, osl=osl: e.tensor_copy(out=axgT[osl][:, :], in_=mk.ps_tr[:, 0:128]))
                mk.wait("t", "v", cv)
                for oc in range(4):
                    pi = gi % 2
                    mk.wait("t", "v", ps_guard[pi])

                    def mmg(eng, sl=sl, tt=tt, oc=oc, pi=pi, osl=osl):
                        r = None
                        for ct in range(CT):
                            eng.matmul(ps[pi][:, :],
                                       xch[sl][:, ct * CW + tt * 128:ct * CW + tt * 128 + 128],
                                       w_sb[:, ct * 2048 + oc * 512:ct * 2048 + oc * 512 + 512],
                                       start=(ct == 0), stop=False)
                        r = eng.matmul(ps[pi][:, :], axgT[osl][:, :],
                                       b_sb[:, oc * 512:(oc + 1) * 512], start=False, stop=True)
                        return r
                    tg2 = mk.group("t", mmg)
                    mk.wait("v", "t", tg2)
                    mk.dwait("v", ot_guard[osl])
                    mk.dwait("v", rx)
                    bv = mk.op("v", lambda e, pi=pi, osl=osl, oc=oc: e.tensor_tensor(
                        out=ot[osl][:, oc * 512:(oc + 1) * 512], in0=ps[pi][:, :],
                        in1=xres[osl][:, oc * 512:(oc + 1) * 512], op=ALU.add))
                    ps_guard[pi] = bv
                    gi += 1
                mk.wait("y", "v")
                ot_guard[osl] = mk.dma(x_out[ti * 128:(ti + 1) * 128, :], ot[osl][:, :], key=f"xo{osl}")
                xres_guard_v[osl] = mk.cnt["v"]
                axgT_guard[osl] = mk.cnt["t"]
            xch_guard[sl] = mk.cnt["t"]
        mk.barrier()


def emit_L3F(mk, x_in, x_out, sc):
    with contextlib.ExitStack() as ctx:
        tok_rln = mk.sb(ctx, "f3_rln", [128, NT], F32)
        with contextlib.ExitStack() as c2:
            emit_ln_pass(mk, c2, x_in, sc["xnT"], tok_rln=tok_rln)
        a_sb = mk.sb(ctx, "f3_a", [128, CT * 1024], BF16)
        b_sb = mk.sb(ctx, "f3_b", [128, 8 * 2048], BF16)
        r_sb = mk.sb(ctx, "f3_r", [128, CT * 8], BF16)
        with contextlib.ExitStack() as c2:
            load_weights_bf16(mk, c2, [(mk.inp("l3f_a"), a_sb), (mk.inp("l3f_b"), b_sb),
                                       (mk.inp("l3f_rw"), r_sb)])
            mk.barrier()
        xch = [mk.sb(ctx, f"f3_x{i}", [128, CT * CW], BF16) for i in range(2)]
        xres = [mk.sb(ctx, f"f3_rs{i}", [128, 2048], F32) for i in range(2)]
        ot = [mk.sb(ctx, f"f3_o{i}", [128, 2048], F32) for i in range(2)]
        colb = mk.sb(ctx, "f3_col", [128, 16], F32)
        gbuf = mk.sb(ctx, "f3_g", [128, 8], F32)
        axg = mk.sb(ctx, "f3_axg", [128, 1024], BF16)
        axgT = [mk.sb(ctx, f"f3_axT{i}", [128, 1024], BF16) for i in range(2)]
        ps = [mk.ps_a, mk.ps_b]
        ps_guard = [0, 0]
        ot_guard = [None, None]
        xch_guard = [0, 0]
        xres_guard_v = [0, 0]
        axgT_guard = [0, 0]
        gi = 0
        for ch in range(NCH):
            sl = ch % 2
            mk.wait("y", "t", xch_guard[sl])
            xt = mk.dma(xch[sl][:, :].rearrange("p (ct w) -> p ct w", w=CW),
                        sc["xnT"].rearrange("(ct p) t -> p ct t", p=128)[:, :, ch * CW:(ch + 1) * CW], key=f"xch{sl}")
            mk.dwait("t", xt)
            for tt in range(CW // 128):
                ti = ch * 4 + tt
                osl = ti % 2
                mk.wait("y", "v", xres_guard_v[osl])
                rx = mk.dma(xres[osl][:, :], x_in[ti * 128:(ti + 1) * 128, :], key=f"xr{osl}")
                mk.wait("t", "v")

                def lg(eng, sl=sl, tt=tt):
                    r = None
                    for ct in range(CT):
                        r = eng.matmul(mk.ps_x[:, 0:8],
                                       xch[sl][:, ct * CW + tt * 128:ct * CW + tt * 128 + 128],
                                       r_sb[:, ct * 8:(ct + 1) * 8],
                                       start=(ct == 0), stop=(ct == CT - 1))
                    for half in range(2):
                        for ct in range(CT):
                            r = eng.matmul(mk.ps_mm[:, half * 512:(half + 1) * 512],
                                           xch[sl][:, ct * CW + tt * 128:ct * CW + tt * 128 + 128],
                                           a_sb[:, ct * 1024 + half * 512:ct * 1024 + half * 512 + 512],
                                           start=(ct == 0), stop=(ct == CT - 1))
                    return r
                tg = mk.group("t", lg)
                mk.wait("v", "t", tg)
                lv = mk.op("v", lambda e, ti=ti: e.tensor_scalar(
                    out=gbuf[:, :], in0=mk.ps_x[:, 0:8], scalar1=tok_rln[:, ti:ti + 1], scalar2=None, op0=ALU.mult))
                mk.wait("v", "v", lv)
                mk.wait("s", "v", lv)
                ev = mk.op("s", lambda e: e.activation(out=gbuf[:, :], in_=gbuf[:, :], func=AF.Exp))
                mk.wait("v", "s", ev)
                m1 = mk.op("v", lambda e: e.reduce_max(out=colb[:, 0:1], in_=gbuf[:, :], axis=AX))
                mk.wait("v", "v", m1)
                ms = mk.op("v", lambda e: e.tensor_scalar(out=colb[:, 8:16], in0=gbuf[:, :], scalar1=colb[:, 0:1], scalar2=None, op0=ALU.is_lt))
                mk.wait("v", "v", ms)
                mk.op("v", lambda e: e.tensor_tensor(out=colb[:, 8:16], in0=colb[:, 8:16], in1=gbuf[:, :], op=ALU.mult))
                mk.wait("v", "v")
                mk.op("v", lambda e: e.reduce_max(out=colb[:, 1:2], in_=colb[:, 8:16], axis=AX))
                mk.wait("v", "v")
                mk.op("v", lambda e: e.tensor_tensor(out=colb[:, 2:3], in0=colb[:, 0:1], in1=colb[:, 1:2], op=ALU.add))
                mk.wait("v", "v")
                mk.op("v", lambda e: e.reciprocal(out=colb[:, 2:3], in_=colb[:, 2:3]))
                mk.wait("v", "v")
                mk.op("v", lambda e: e.tensor_scalar(out=colb[:, 8:16], in0=gbuf[:, :], scalar1=colb[:, 1:2], scalar2=None, op0=ALU.is_ge))
                mk.wait("v", "v")
                mk.op("v", lambda e: e.tensor_tensor(out=gbuf[:, :], in0=gbuf[:, :], in1=colb[:, 8:16], op=ALU.mult))
                mk.wait("v", "v")
                mk.op("v", lambda e: e.tensor_scalar(out=gbuf[:, :], in0=gbuf[:, :], scalar1=colb[:, 2:3], scalar2=None, op0=ALU.mult))
                mk.wait("v", "v")
                gx = gbuf[:, :].unsqueeze(2).broadcast_to([128, 8, 128])
                ax = mk.op("v", lambda e: e.tensor_tensor(
                    out=axg[:, :].rearrange("p (e r) -> p e r", e=8),
                    in0=mk.ps_mm[:, 0:1024].rearrange("p (e r) -> p e r", e=8), in1=gx, op=ALU.mult))
                mk.wait("t", "v", ax)
                tr = mk.group("t", lambda e: [e.transpose(mk.ps_tr[:, j * 128:(j + 1) * 128], axg[:, j * 128:(j + 1) * 128], mk.idb[:, :]) for j in range(8)][-1])
                mk.wait("v", "t", tr)
                mk.wait("v", "t", axgT_guard[osl])
                cv = mk.op("v", lambda e, osl=osl: e.tensor_copy(out=axgT[osl][:, :], in_=mk.ps_tr[:, :]))
                mk.wait("t", "v", cv)
                for oc in range(4):
                    pi = gi % 2
                    mk.wait("t", "v", ps_guard[pi])

                    def mmg(eng, oc=oc, pi=pi, osl=osl):
                        r = None
                        for kt in range(8):
                            r = eng.matmul(ps[pi][:, :], axgT[osl][:, kt * 128:(kt + 1) * 128],
                                           b_sb[:, kt * 2048 + oc * 512:kt * 2048 + oc * 512 + 512],
                                           start=(kt == 0), stop=(kt == 7))
                        return r
                    tg2 = mk.group("t", mmg)
                    mk.wait("v", "t", tg2)
                    mk.dwait("v", ot_guard[osl])
                    mk.dwait("v", rx)
                    bv = mk.op("v", lambda e, pi=pi, osl=osl, oc=oc: e.tensor_tensor(
                        out=ot[osl][:, oc * 512:(oc + 1) * 512], in0=ps[pi][:, :],
                        in1=xres[osl][:, oc * 512:(oc + 1) * 512], op=ALU.add))
                    ps_guard[pi] = bv
                    gi += 1
                mk.wait("y", "v")
                ot_guard[osl] = mk.dma(x_out[ti * 128:(ti + 1) * 128, :], ot[osl][:, :], key=f"xo{osl}")
                xres_guard_v[osl] = mk.cnt["v"]
            xch_guard[sl] = mk.cnt["t"]
        mk.barrier()


def emit_L2F(mk, x_in, x_out, sc):
    CWL = 1024
    NCHL = NTOK // CWL
    NTTL = CWL // 128
    with contextlib.ExitStack() as ctx:
        emit_ln_pass(mk, ctx, x_in, sc["xnT"])
    ew_bf = mk.scratch("l2f_ewb", [8 * 2048, 2048], BF16)
    ow_bf = mk.scratch("l2f_owb", [2048, 2048], BF16)
    with contextlib.ExitStack() as ctx:
        stg = [mk.sb(ctx, f"f2cv{i}", [128, 2048], F32) for i in range(2)]
        obf = [mk.sb(ctx, f"f2cb{i}", [128, 2048], BF16) for i in range(2)]
        stg_g = [None, None]
        ew = mk.inp("l2f_ew")
        ow = mk.inp("l2f_ow")
        jobs = [(ew[rt // CT, (rt % CT) * 128:((rt % CT) + 1) * 128, :], ew_bf[rt * 128:(rt + 1) * 128, :])
                for rt in range(8 * CT)]
        jobs += [(ow[rt * 128:(rt + 1) * 128, :], ow_bf[rt * 128:(rt + 1) * 128, :]) for rt in range(CT)]
        for j, (sap, dap) in enumerate(jobs):
            slq = j % 2
            ee = "v" if slq == 0 else "g"
            mk.dwait("y", stg_g[slq])
            t = mk.dma(stg[slq][:, :], sap, key=f"wld{slq}")
            mk.dwait(ee, t)
            cv = mk.op(ee, lambda e, slq=slq: e.tensor_copy(out=obf[slq][:, :], in_=stg[slq][:, :]))
            mk.wait("y", ee, cv)
            stg_g[slq] = mk.dma(dap, obf[slq][:, :], key=f"wst{slq}")
        mk.barrier()
    with contextlib.ExitStack() as ctx:
        gates = mk.sb(ctx, "f2_gts", [128, NT * 8], F32)
        colb = mk.sb(ctx, "f2_col", [128, 16], F32)
        rbr = mk.sb(ctx, "f2_rbr", [1, 8], F32)
        rbb = mk.sb(ctx, "f2_rbb", [128, 8], F32)
        t = mk.dma(rbr[:, :], mk.inp("l2f_rb")[:, :], key="c")
        mk.dwait("t", t)
        mk.dwait("v", t)
        mk.bcast_row(rbr, 8, rbb)
        with contextlib.ExitStack() as c2:
            r_sb = mk.sb(c2, "f2_r", [128, CT * 8], BF16)
            with contextlib.ExitStack() as c3:
                load_weights_bf16(mk, c3, [(mk.inp("l2f_rw"), r_sb)])
                mk.barrier()
            xch2 = [mk.sb(c2, f"f2_x{i}", [128, CT * CW], BF16) for i in range(2)]
            xch_guard = [0, 0]
            for ch in range(NCH):
                sl = ch % 2
                mk.wait("y", "t", xch_guard[sl])
                xt = mk.dma(xch2[sl][:, :].rearrange("p (ct w) -> p ct w", w=CW),
                            sc["xnT"].rearrange("(ct p) t -> p ct t", p=128)[:, :, ch * CW:(ch + 1) * CW], key=f"xch{sl}")
                mk.dwait("t", xt)
                for tt in range(CW // 128):
                    ti = ch * 4 + tt
                    mk.wait("t", "v")

                    def lg(eng, sl=sl, tt=tt):
                        r = None
                        for ct in range(CT):
                            r = eng.matmul(mk.ps_x[:, 0:8],
                                           xch2[sl][:, ct * CW + tt * 128:ct * CW + tt * 128 + 128],
                                           r_sb[:, ct * 8:(ct + 1) * 8],
                                           start=(ct == 0), stop=(ct == CT - 1))
                        return r
                    tg = mk.group("t", lg)
                    mk.wait("v", "t", tg)
                    _router_top2(mk, colb, mk.ps_x[:, 0:8], gates[:, ti * 8:(ti + 1) * 8], bias_bc=rbb)
                xch_guard[sl] = mk.cnt["t"]
            mk.barrier()
        obr = mk.sb(ctx, "f2_obr", [1, 2048], F32)
        obb = mk.sb(ctx, "f2_obb", [128, 2048], F32)
        t = mk.dma(obr[:, :], mk.inp("l2f_ob")[:, :], key="c")
        mk.dwait("t", t)
        mk.dwait("v", t)
        mk.bcast_row(obr, 2048, obb)
        xch = mk.sb(ctx, "f2_xc", [128, CT * CWL], BF16)
        acc = mk.sb(ctx, "f2_acc", [128, NTTL * 2048], F32)
        ber = mk.sb(ctx, "f2_ber", [1, 2048], F32)
        beb = mk.sb(ctx, "f2_beb", [128, 2048], F32)
        wgrp = [mk.sb(ctx, f"f2_wg{i}", [128, CT * 512], BF16) for i in range(2)]
        t1b = mk.sb(ctx, "f2_t1", [128, 512], F32)
        t2b = mk.sb(ctx, "f2_t2", [128, 512], F32)
        accb = mk.sb(ctx, "f2_ab", [128, 2048], BF16)
        accT = mk.sb(ctx, "f2_aT", [128, 2048], BF16)
        xres = mk.sb(ctx, "f2_rs", [128, 2048], F32)
        ot = mk.sb(ctx, "f2_o0", [128, 2048], F32)
        ps = [mk.ps_a, mk.ps_b]
        ot_guard = None
        xch_tok = None
        wgrp_guard = [0, 0]
        for ch in range(NCHL):
            mk.wait("y", "t")
            xch_tok = mk.dma(xch[:, :].rearrange("p (ct w) -> p ct w", w=CWL),
                             sc["xnT"].rearrange("(ct p) t -> p ct t", p=128)[:, :, ch * CWL:(ch + 1) * CWL], key="xchL")
            mk.dwait("t", xch_tok)
            ps_guard = [0, 0]
            wgrp_tok = [None] * 2
            gi = 0
            for e_ in range(8):
                tb = mk.dma(ber[:, :], mk.inp("l2f_eb")[e_, :, :], key="c2")
                mk.dwait("t", tb)
                mk.dwait("v", tb)
                mk.wait("t", "v")
                mk.bcast_row(ber, 2048, beb)
                for oc in range(4):
                    wsl = oc % 2
                    mk.wait("y", "t", wgrp_guard[wsl])
                    wgrp_tok[wsl] = mk.dma(
                        wgrp[wsl][:, :].rearrange("p (ct w) -> p ct w", w=512),
                        ew_bf.rearrange("(e ct p) c -> p e ct c", ct=CT, p=128)[:, e_, :, oc * 512:(oc + 1) * 512],
                        key=f"wgl{wsl}")
                    for tt in range(NTTL):
                        pi = gi % 2
                        mk.wait("t", "v", ps_guard[pi])
                        mk.dwait("t", wgrp_tok[wsl])

                        def mmg(eng, tt=tt, wsl=wsl, pi=pi):
                            r = None
                            for ct in range(CT):
                                r = eng.matmul(ps[pi][:, :],
                                               xch[:, ct * CWL + tt * 128:ct * CWL + tt * 128 + 128],
                                               wgrp[wsl][:, ct * 512:(ct + 1) * 512],
                                               start=(ct == 0), stop=(ct == CT - 1))
                            return r
                        tg = mk.group("t", mmg)
                        mk.wait("v", "t", tg)
                        v1 = mk.op("v", lambda e, pi=pi, oc=oc: e.tensor_tensor(
                            out=t1b[:, :], in0=ps[pi][:, :], in1=beb[:, oc * 512:(oc + 1) * 512], op=ALU.add))
                        ps_guard[pi] = v1
                        mk.wait("g", "v", v1)
                        mk.wait("g", "s")
                        mk.op("g", lambda e: e.tensor_tensor(out=t2b[:, :], in0=t1b[:, :], in1=t1b[:, :], op=ALU.mult))
                        mk.wait("g", "g")
                        mk.op("g", lambda e: e.tensor_scalar(out=t2b[:, :], in0=t2b[:, :], scalar1=0.7978845608028654 * 0.044715, scalar2=0.7978845608028654, op0=ALU.mult, op1=ALU.add))
                        mk.wait("g", "g")
                        g2 = mk.op("g", lambda e: e.tensor_tensor(out=t2b[:, :], in0=t2b[:, :], in1=t1b[:, :], op=ALU.mult))
                        mk.wait("s", "g", g2)
                        s1_ = mk.op("s", lambda e: e.activation(out=t2b[:, :], in_=t2b[:, :], func=AF.Tanh))
                        mk.wait("v", "s", s1_)
                        mk.op("v", lambda e: e.tensor_scalar(out=t2b[:, :], in0=t2b[:, :], scalar1=0.5, scalar2=0.5, op0=ALU.mult, op1=ALU.add))
                        mk.wait("v", "v")
                        gl = mk.op("v", lambda e: e.tensor_tensor(out=t2b[:, :], in0=t2b[:, :], in1=t1b[:, :], op=ALU.mult))
                        mk.wait("v", "v", gl)
                        gcol = (ch * NTTL + tt) * 8 + e_
                        if e_ == 0:
                            av = mk.op("v", lambda e, tt=tt, oc=oc, gcol=gcol: e.tensor_scalar(
                                out=acc[:, tt * 2048 + oc * 512:tt * 2048 + oc * 512 + 512], in0=t2b[:, :],
                                scalar1=gates[:, gcol:gcol + 1], scalar2=None, op0=ALU.mult))
                        else:
                            av = mk.op("v", lambda e, tt=tt, oc=oc, gcol=gcol: e.scalar_tensor_tensor(
                                out=acc[:, tt * 2048 + oc * 512:tt * 2048 + oc * 512 + 512], in0=t2b[:, :],
                                scalar=gates[:, gcol:gcol + 1],
                                in1=acc[:, tt * 2048 + oc * 512:tt * 2048 + oc * 512 + 512],
                                op0=ALU.mult, op1=ALU.add))
                        gi += 1
                    wgrp_guard[wsl] = mk.cnt["t"]
            for tt in range(NTTL):
                ti = ch * NTTL + tt
                osl = tt % 2
                mk.wait("y", "v")
                rx = mk.dma(xres[:, :], x_in[ti * 128:(ti + 1) * 128, :], key="xr0")
                mk.wait("v", "t")
                cb = mk.op("v", lambda e, tt=tt: e.tensor_copy(out=accb[:, :], in_=acc[:, tt * 2048:(tt + 1) * 2048]))
                mk.wait("t", "v", cb)
                for half in range(2):
                    def trg(eng, half=half):
                        r = None
                        for j in range(8):
                            r = eng.transpose(mk.ps_tr[:, j * 128:(j + 1) * 128],
                                              accb[:, (half * 8 + j) * 128:(half * 8 + j + 1) * 128], mk.idb[:, :])
                        return r
                    tg0 = mk.group("t", trg)
                    mk.wait("v", "t", tg0)
                    cvt = mk.op("v", lambda e, half=half: e.tensor_copy(
                        out=accT[:, half * 1024:(half + 1) * 1024], in_=mk.ps_tr[:, :]))
                    mk.wait("t", "v", cvt)
                for oc in range(4):
                    pi = oc % 2
                    wsl = oc % 2
                    mk.wait("y", "t", wgrp_guard[wsl])
                    wgrp_tok[wsl] = mk.dma(
                        wgrp[wsl][:, :].rearrange("p (ct w) -> p ct w", w=512),
                        ow_bf.rearrange("(ct p) c -> p ct c", p=128)[:, :, oc * 512:(oc + 1) * 512],
                        key=f"wgl{wsl}")
                    mk.wait("t", "v", ps_guard[pi])
                    mk.dwait("t", wgrp_tok[wsl])

                    def mmg2(eng, pi=pi, wsl=wsl):
                        r = None
                        for ct in range(CT):
                            r = eng.matmul(ps[pi][:, :], accT[:, ct * 128:(ct + 1) * 128],
                                           wgrp[wsl][:, ct * 512:(ct + 1) * 512],
                                           start=(ct == 0), stop=(ct == CT - 1))
                        return r
                    tg2 = mk.group("t", mmg2)
                    wgrp_guard[wsl] = mk.cnt["t"]
                    mk.wait("v", "t", tg2)
                    mk.dwait("v", ot_guard)
                    mk.dwait("v", rx)
                    mk.op("v", lambda e, pi=pi, oc=oc: e.tensor_tensor(
                        out=ot[:, oc * 512:(oc + 1) * 512], in0=ps[pi][:, :],
                        in1=obb[:, oc * 512:(oc + 1) * 512], op=ALU.add))
                    mk.wait("v", "v")
                    bv = mk.op("v", lambda e, pi=pi, oc=oc: e.tensor_tensor(
                        out=ot[:, oc * 512:(oc + 1) * 512], in0=ot[:, oc * 512:(oc + 1) * 512],
                        in1=xres[:, oc * 512:(oc + 1) * 512], op=ALU.add))
                    ps_guard[pi] = bv
                mk.wait("y", "v")
                ot_guard = mk.dma(x_out[ti * 128:(ti + 1) * 128, :], ot[:, :], key="xo0")
        mk.barrier()


def _dequant_stream(mk, ctx, pk_ap, sc_ap, out_s, njt, nnc, scale_rows, down=False):
    pk = [mk.sb(ctx, f"dq_p{i}", [128, 512], I32) for i in range(2)]
    se = [mk.sb(ctx, f"dq_s{i}", [128, 512], F32) for i in range(2)]
    mi = mk.sb(ctx, "dq_m", [128, 512], I32)
    mf = mk.sb(ctx, "dq_mf", [128, 512], F32)
    a1 = mk.sb(ctx, "dq_a1", [128, 512], F32)
    a2 = mk.sb(ctx, "dq_a2", [128, 512], F32)
    mn = mk.sb(ctx, "dq_mn", [128, 512], F32)
    si = mk.sb(ctx, "dq_si", [128, 512], I32)
    qv = mk.sb(ctx, "dq_qv", [128, 512], F32)
    wpl = [mk.sb(ctx, f"dq_w{i}", [128, 512], BF16) for i in range(2)]
    ld_guard = [0, 0]
    st_guard = [None, None]
    ncols = nnc * 512
    pi = 0
    for jt in range(njt):
        for nc_ in range(nnc):
            sl = pi % 2
            mk.wait("y", "v", ld_guard[sl])
            tp = mk.dma(pk[sl][:, :], pk_ap[jt * 128:(jt + 1) * 128, nc_ * 512:(nc_ + 1) * 512], key=f"dqp{sl}")
            sap = bass.AP(sc_ap.tensor, (8 * jt) * ncols + nc_ * 512,
                          [[ncols, 8], [0, 16], [1, 512]])
            ts_ = mk.dma(se[sl][:, :], sap, key=f"dqs{sl}")
            mk.dwait("v", tp)
            mk.dwait("v", ts_)
            for off in range(2):
                wsl = off
                if off == 0:
                    v0 = mk.op("v", lambda e, sl=sl: e.tensor_scalar(out=mi[:, :], in0=pk[sl][:, :], scalar1=7, scalar2=None, op0=ALU.bitwise_and))
                    mk.wait("v", "v", v0)
                    vs = mk.op("v", lambda e, sl=sl: e.tensor_scalar(out=si[:, :], in0=pk[sl][:, :], scalar1=8, scalar2=3, op0=ALU.bitwise_and, op1=ALU.logical_shift_right))
                else:
                    v0 = mk.op("v", lambda e, sl=sl: e.tensor_scalar(out=mi[:, :], in0=pk[sl][:, :], scalar1=4, scalar2=7, op0=ALU.logical_shift_right, op1=ALU.bitwise_and))
                    mk.wait("v", "v", v0)
                    vs = mk.op("v", lambda e, sl=sl: e.tensor_scalar(out=si[:, :], in0=pk[sl][:, :], scalar1=128, scalar2=7, op0=ALU.bitwise_and, op1=ALU.logical_shift_right))
                mk.wait("g", "v", v0)
                g3 = mk.op("g", lambda e: e.tensor_copy(out=mf[:, :], in_=mi[:, :]))
                mk.wait("g", "g", g3)
                g4 = mk.op("g", lambda e: e.tensor_scalar(out=a1[:, :], in0=mf[:, :], scalar1=-2.0, scalar2=4.0, op0=ALU.mult, op1=ALU.add))
                mk.wait("g", "g", g4)
                g5 = mk.op("g", lambda e: e.tensor_scalar(out=a2[:, :], in0=mf[:, :], scalar1=-4.0, scalar2=16.0, op0=ALU.mult, op1=ALU.add))
                mk.wait("g", "g", g5)
                mk.wait("v", "g", g5)
                g6 = mk.op("v", lambda e: e.tensor_tensor(out=mn[:, :], in0=a1[:, :], in1=a2[:, :], op=ALU.min))
                mk.wait("v", "v", g6)
                v3 = mk.op("v", lambda e: e.scalar_tensor_tensor(out=mn[:, :], in0=mf[:, :], scalar=-1.0, in1=mn[:, :], op0=ALU.mult, op1=ALU.min))
                mk.wait("v", "v", vs)
                v4 = mk.op("v", lambda e, sl=sl: e.scalar_tensor_tensor(out=qv[:, :], in0=si[:, :], scalar=0.5, in1=se[sl][:, :], op0=ALU.subtract, op1=ALU.mult))
                mk.wait("g", "v", v4)
                mk.dwait("g", st_guard[wsl])
                g7 = mk.op("g", lambda e, wsl=wsl: e.tensor_tensor(out=wpl[wsl][:, :], in0=mn[:, :], in1=qv[:, :], op=ALU.mult))
                mk.wait("y", "g", g7)
                if not down:
                    kt = 2 * jt + off
                    st_guard[wsl] = mk.dma(out_s[kt * 128:(kt + 1) * 128, nc_ * 512:(nc_ + 1) * 512], wpl[wsl][:, :], key=f"dqo{wsl}")
                else:
                    kt0, kt1 = 2 * jt, 2 * jt + 1
                    mk.dma(out_s[kt0 * 128 + off * 64:kt0 * 128 + off * 64 + 64, nc_ * 512:(nc_ + 1) * 512], wpl[wsl][0:64, :], key=f"dqo{wsl}")
                    st_guard[wsl] = mk.dma(out_s[kt1 * 128 + off * 64:kt1 * 128 + off * 64 + 64, nc_ * 512:(nc_ + 1) * 512], wpl[wsl][64:128, :], key=f"dqo{wsl}")
            ld_guard[sl] = mk.cnt["v"]
            pi += 1
    mk.barrier()


def emit_L0F(mk, x_in, x_out, sc):
    with contextlib.ExitStack() as ctx:
        emit_ln_pass(mk, ctx, x_in, sc["xnT"], pi_order=True)
    wg_s = mk.scratch("l0f_wg", [2048, 16384], BF16)
    wu_s = mk.scratch("l0f_wu", [2048, 16384], BF16)
    wd_s = mk.scratch("l0f_wd", [16384, 2048], BF16)
    with contextlib.ExitStack() as ctx:
        _dequant_stream(mk, ctx, mk.inp("l0_pg"), mk.inp("l0_sg"), wg_s, 8, 32, 64)
    with contextlib.ExitStack() as ctx:
        _dequant_stream(mk, ctx, mk.inp("l0_pu"), mk.inp("l0_su"), wu_s, 8, 32, 64)
    with contextlib.ExitStack() as ctx:
        _dequant_stream(mk, ctx, mk.inp("l0_pd"), mk.inp("l0_sd"), wd_s, 64, 4, 512, down=True)
    with contextlib.ExitStack() as ctx:
        xch = mk.sb(ctx, "f0_xc", [128, CT * CW], BF16)
        inter = mk.sb(ctx, "f0_in", [128, 64 * 512], BF16)
        acc = mk.sb(ctx, "f0_ac", [128, 4 * 2048], F32)
        wgb = [mk.sb(ctx, f"f0_wg{i}", [128, 2048], BF16) for i in range(2)]
        wub = [mk.sb(ctx, f"f0_wu{i}", [128, 2048], BF16) for i in range(2)]
        wdb = [mk.sb(ctx, f"f0_wd{i}", [128, 4 * 2048], BF16) for i in range(2)]
        g1b = mk.sb(ctx, "f0_g1", [128, 512], F32)
        t2b = mk.sb(ctx, "f0_t2", [128, 512], F32)
        xres = mk.sb(ctx, "f0_rs", [128, 2048], F32)
        ot = mk.sb(ctx, "f0_ot", [128, 2048], F32)
        wg_r = wg_s.rearrange("(kt p) n -> p kt n", p=128)
        wu_r = wu_s.rearrange("(kt p) n -> p kt n", p=128)
        wd_r = wd_s.rearrange("(kt p) o -> p kt o", p=128)
        ps_ga = [0, 0]
        wld_guard = [0, 0]
        wdd_guard = [0, 0]
        ot_guard = None
        xr_guard = 0
        for ch in range(NCH):
            mk.wait("y", "t")
            xt = mk.dma(xch[:, :].rearrange("p (ct w) -> p ct w", w=CW),
                        sc["xnT"].rearrange("(ct p) t -> p ct t", p=128)[:, :, ch * CW:(ch + 1) * CW], key="xchL")
            mk.dwait("t", xt)
            first_acc = True
            il_guard = 0
            for half in range(2):
                for nt in range(64):
                    ntg = half * 64 + nt
                    sl = nt % 2
                    mk.wait("y", "t", wld_guard[sl])
                    tw1 = mk.dma(wgb[sl][:, :].rearrange("p (ct n) -> p ct n", n=128),
                                 wg_r[:, :, ntg * 128:(ntg + 1) * 128], key=f"wgl{sl}")
                    tw2 = mk.dma(wub[sl][:, :].rearrange("p (ct n) -> p ct n", n=128),
                                 wu_r[:, :, ntg * 128:(ntg + 1) * 128], key=f"wul{sl}")
                    mk.dwait("t", tw1)
                    mk.dwait("t", tw2)
                    mk.wait("t", "v", ps_ga[0])

                    def mg(eng, sl=sl):
                        r = None
                        for ct in range(CT):
                            r = eng.matmul(mk.ps_a[:, :], wgb[sl][:, ct * 128:(ct + 1) * 128],
                                           xch[:, ct * CW:(ct + 1) * CW], start=(ct == 0), stop=(ct == CT - 1))
                        return r
                    tga = mk.group("t", mg)
                    mk.wait("t", "v", ps_ga[1])

                    def mu(eng, sl=sl):
                        r = None
                        for ct in range(CT):
                            r = eng.matmul(mk.ps_b[:, :], wub[sl][:, ct * 128:(ct + 1) * 128],
                                           xch[:, ct * CW:(ct + 1) * CW], start=(ct == 0), stop=(ct == CT - 1))
                        return r
                    tgb = mk.group("t", mu)
                    wld_guard[sl] = mk.cnt["t"]
                    mk.wait("v", "t", tga)
                    v1 = mk.op("v", lambda e: e.tensor_copy(out=g1b[:, :], in_=mk.ps_a[:, :]))
                    ps_ga[0] = v1
                    mk.wait("g", "v", v1)
                    mk.wait("g", "s")
                    mk.op("g", lambda e: e.tensor_tensor(out=t2b[:, :], in0=g1b[:, :], in1=g1b[:, :], op=ALU.mult))
                    mk.wait("g", "g")
                    mk.op("g", lambda e: e.tensor_scalar(out=t2b[:, :], in0=t2b[:, :], scalar1=0.7978845608028654 * 0.044715, scalar2=0.7978845608028654, op0=ALU.mult, op1=ALU.add))
                    mk.wait("g", "g")
                    gg = mk.op("g", lambda e: e.tensor_tensor(out=t2b[:, :], in0=t2b[:, :], in1=g1b[:, :], op=ALU.mult))
                    mk.wait("s", "g", gg)
                    s1_ = mk.op("s", lambda e: e.activation(out=t2b[:, :], in_=t2b[:, :], func=AF.Tanh))
                    mk.wait("v", "s", s1_)
                    mk.op("v", lambda e: e.tensor_scalar(out=t2b[:, :], in0=t2b[:, :], scalar1=0.5, scalar2=0.5, op0=ALU.mult, op1=ALU.add))
                    mk.wait("v", "v")
                    mk.op("v", lambda e: e.tensor_tensor(out=t2b[:, :], in0=t2b[:, :], in1=g1b[:, :], op=ALU.mult))
                    mk.wait("v", "v")
                    mk.wait("v", "t", il_guard)
                    mk.wait("v", "t", tgb)
                    iv = mk.op("v", lambda e, nt=nt: e.tensor_tensor(out=inter[:, nt * 512:(nt + 1) * 512], in0=t2b[:, :], in1=mk.ps_b[:, :], op=ALU.mult))
                    ps_ga[1] = iv
                    mk.wait("t", "v", iv)
                mk.wait("t", "v")
                for ktb in range(16):
                    sl = ktb % 2
                    mk.wait("y", "t", wdd_guard[sl])
                    ktg = half * 64 + ktb * 4
                    twd = mk.dma(wdb[sl][:, :].rearrange("p (j o) -> p j o", o=2048),
                                 wd_r[:, ktg:ktg + 4, :], key=f"wdl{sl}")
                    mk.dwait("t", twd)
                    for tt in range(4):
                        for oc in range(4):
                            pi2 = (tt * 4 + oc) % 2
                            psd = mk.ps_a if pi2 == 0 else mk.ps_b
                            mk.wait("t", "v", ps_ga[pi2])

                            def md(eng, sl=sl, ktb=ktb, tt=tt, oc=oc, psd=psd):
                                r = None
                                for j in range(4):
                                    r = eng.matmul(psd[:, :],
                                                   inter[:, (ktb * 4 + j) * 512 + tt * 128:(ktb * 4 + j) * 512 + tt * 128 + 128],
                                                   wdb[sl][:, j * 2048 + oc * 512:j * 2048 + oc * 512 + 512],
                                                   start=(j == 0), stop=(j == 3))
                                return r
                            tgd = mk.group("t", md)
                            mk.wait("v", "t", tgd)
                            if first_acc:
                                av = mk.op("v", lambda e, tt=tt, oc=oc, psd=psd: e.tensor_copy(
                                    out=acc[:, tt * 2048 + oc * 512:tt * 2048 + oc * 512 + 512], in_=psd[:, :]))
                            else:
                                av = mk.op("v", lambda e, tt=tt, oc=oc, psd=psd: e.tensor_tensor(
                                    out=acc[:, tt * 2048 + oc * 512:tt * 2048 + oc * 512 + 512], in0=psd[:, :],
                                    in1=acc[:, tt * 2048 + oc * 512:tt * 2048 + oc * 512 + 512], op=ALU.add))
                            ps_ga[pi2] = av
                    first_acc = False
                    wdd_guard[sl] = mk.cnt["t"]
                il_guard = mk.cnt["t"]
            for tt in range(4):
                ti = ch * 4 + tt
                mk.wait("y", "v", xr_guard)
                rx = mk.dma(xres[:, :], x_in[ti * 128:(ti + 1) * 128, :], key="xr0")
                mk.dwait("v", rx)
                mk.dwait("v", ot_guard)
                mk.wait("v", "v")
                fv = mk.op("v", lambda e, tt=tt: e.tensor_tensor(out=ot[:, :], in0=acc[:, tt * 2048:(tt + 1) * 2048], in1=xres[:, :], op=ALU.add))
                xr_guard = fv
                mk.wait("y", "v", fv)
                ot_guard = mk.dma(x_out[ti * 128:(ti + 1) * 128, :], ot[:, :], key="xo0")
        mk.barrier()


def emit_L1A(mk, x_in, x_out, sc):
    with contextlib.ExitStack() as ctx:
        emit_ln_pass(mk, ctx, x_in, sc["xnT"])
    with contextlib.ExitStack() as ctx:
        wpool = mk.sb(ctx, "l1_w", [128, CT * 2048], BF16)
        bq = mk.sb(ctx, "l1_bq", [128, 16], F32)
        bk = mk.sb(ctx, "l1_bk", [128, 16], F32)
        bvr = mk.sb(ctx, "l1_bvr", [1, 2048], F32)
        bvb = mk.sb(ctx, "l1_bvb", [128, 2048], F32)
        mk.dma(bq[:, :], mk.inp("l1_bq_c")[:, :], key="c")
        mk.dma(bk[:, :], mk.inp("l1_bk_c")[:, :], key="c")
        t = mk.dma(bvr[:, :], mk.inp("l1_bv_r")[:, :], key="c")
        mk.dwait("t", t)
        mk.dwait("v", t)
        mk.bcast_row(bvr, 2048, bvb)
        mk.barrier()
        with contextlib.ExitStack() as c2:
            load_weights_bf16(mk, c2, [(mk.inp("l1_wq"), wpool)])
            mk.barrier()
        with contextlib.ExitStack() as c2:
            proj_dt(mk, c2, sc["xnT"], wpool, CT, sc["q"], bias_col_sb=bq)
        with contextlib.ExitStack() as c2:
            load_weights_bf16(mk, c2, [(mk.inp("l1_wk"), wpool)])
            mk.barrier()
        with contextlib.ExitStack() as c2:
            proj_dt(mk, c2, sc["xnT"], wpool, CT, sc["k"], bias_col_sb=bk)
        with contextlib.ExitStack() as c2:
            load_weights_bf16(mk, c2, [(mk.inp("l1_wv"), wpool)])
            mk.barrier()
        with contextlib.ExitStack() as c2:
            proj_td(mk, c2, sc["xnT"], wpool, 2048, sc["v"], bias_bc=bvb)
    with contextlib.ExitStack() as ctx:
        attn_core(mk, ctx, dict(
            nheads=16, hd=128, kv_group=lambda h: h, n_kv=16,
            q_s=sc["q"], k_s=sc["k"], v_s=sc["v"], ctx_s=sc["c"],
            causal=True, scale=RT128, alibi_slopes=ALIBI_SLOPES))
    with contextlib.ExitStack() as ctx:
        wo_sb = mk.sb(ctx, "l1_wo", [128, CT * 2048], BF16)
        bor = mk.sb(ctx, "l1_bor", [1, 2048], F32)
        bob = mk.sb(ctx, "l1_bob", [128, 2048], F32)
        t = mk.dma(bor[:, :], mk.inp("l1_bo_r")[:, :], key="c")
        mk.dwait("t", t)
        mk.dwait("v", t)
        mk.bcast_row(bor, 2048, bob)
        load_weights_bf16(mk, ctx, [(mk.inp("l1_wo"), wo_sb)])
        mk.barrier()
        out_proj(mk, ctx, sc["c"], wo_sb, x_in, x_out, bob)


def _dequant_stream(mk, ctx, pk_ap, sc_ap, out_s, njt, nnc, scale_rows, down=False):
    pk = [mk.sb(ctx, f"dq_p{i}", [128, 512], I32) for i in range(2)]
    se = [mk.sb(ctx, f"dq_s{i}", [128, 512], F32) for i in range(2)]
    mi = mk.sb(ctx, "dq_m", [128, 512], I32)
    mf = mk.sb(ctx, "dq_mf", [128, 512], F32)
    a1 = mk.sb(ctx, "dq_a1", [128, 512], F32)
    a2 = mk.sb(ctx, "dq_a2", [128, 512], F32)
    mn = mk.sb(ctx, "dq_mn", [128, 512], F32)
    si = mk.sb(ctx, "dq_si", [128, 512], I32)
    qv = mk.sb(ctx, "dq_qv", [128, 512], F32)
    wpl = [mk.sb(ctx, f"dq_w{i}", [128, 512], BF16) for i in range(2)]
    ld_guard = [0, 0]
    st_guard = [None, None]
    ncols = nnc * 512
    pi = 0
    for jt in range(njt):
        for nc_ in range(nnc):
            sl = pi % 2
            mk.wait("y", "v", ld_guard[sl])
            tp = mk.dma(pk[sl][:, :], pk_ap[jt * 128:(jt + 1) * 128, nc_ * 512:(nc_ + 1) * 512], key=f"dqp{sl}")
            sap = bass.AP(sc_ap.tensor, (8 * jt) * ncols + nc_ * 512,
                          [[ncols, 8], [0, 16], [1, 512]])
            ts_ = mk.dma(se[sl][:, :], sap, key=f"dqs{sl}")
            mk.dwait("v", tp)
            mk.dwait("v", ts_)
            for off in range(2):
                wsl = off
                if off == 0:
                    v0 = mk.op("v", lambda e, sl=sl: e.tensor_scalar(out=mi[:, :], in0=pk[sl][:, :], scalar1=7, scalar2=None, op0=ALU.bitwise_and))
                    mk.wait("v", "v", v0)
                    vs = mk.op("v", lambda e, sl=sl: e.tensor_scalar(out=si[:, :], in0=pk[sl][:, :], scalar1=8, scalar2=3, op0=ALU.bitwise_and, op1=ALU.logical_shift_right))
                else:
                    v0 = mk.op("v", lambda e, sl=sl: e.tensor_scalar(out=mi[:, :], in0=pk[sl][:, :], scalar1=4, scalar2=7, op0=ALU.logical_shift_right, op1=ALU.bitwise_and))
                    mk.wait("v", "v", v0)
                    vs = mk.op("v", lambda e, sl=sl: e.tensor_scalar(out=si[:, :], in0=pk[sl][:, :], scalar1=128, scalar2=7, op0=ALU.bitwise_and, op1=ALU.logical_shift_right))
                mk.wait("g", "v", v0)
                g3 = mk.op("g", lambda e: e.tensor_copy(out=mf[:, :], in_=mi[:, :]))
                mk.wait("g", "g", g3)
                g4 = mk.op("g", lambda e: e.tensor_scalar(out=a1[:, :], in0=mf[:, :], scalar1=-2.0, scalar2=4.0, op0=ALU.mult, op1=ALU.add))
                mk.wait("g", "g", g4)
                g5 = mk.op("g", lambda e: e.tensor_scalar(out=a2[:, :], in0=mf[:, :], scalar1=-4.0, scalar2=16.0, op0=ALU.mult, op1=ALU.add))
                mk.wait("g", "g", g5)
                mk.wait("v", "g", g5)
                g6 = mk.op("v", lambda e: e.tensor_tensor(out=mn[:, :], in0=a1[:, :], in1=a2[:, :], op=ALU.min))
                mk.wait("v", "v", g6)
                v3 = mk.op("v", lambda e: e.scalar_tensor_tensor(out=mn[:, :], in0=mf[:, :], scalar=-1.0, in1=mn[:, :], op0=ALU.mult, op1=ALU.min))
                mk.wait("v", "v", vs)
                v4 = mk.op("v", lambda e, sl=sl: e.scalar_tensor_tensor(out=qv[:, :], in0=si[:, :], scalar=0.5, in1=se[sl][:, :], op0=ALU.subtract, op1=ALU.mult))
                mk.wait("g", "v", v4)
                mk.dwait("g", st_guard[wsl])
                g7 = mk.op("g", lambda e, wsl=wsl: e.tensor_tensor(out=wpl[wsl][:, :], in0=mn[:, :], in1=qv[:, :], op=ALU.mult))
                mk.wait("y", "g", g7)
                if not down:
                    kt = 2 * jt + off
                    st_guard[wsl] = mk.dma(out_s[kt * 128:(kt + 1) * 128, nc_ * 512:(nc_ + 1) * 512], wpl[wsl][:, :], key=f"dqo{wsl}")
                else:
                    kt0, kt1 = 2 * jt, 2 * jt + 1
                    mk.dma(out_s[kt0 * 128 + off * 64:kt0 * 128 + off * 64 + 64, nc_ * 512:(nc_ + 1) * 512], wpl[wsl][0:64, :], key=f"dqo{wsl}")
                    st_guard[wsl] = mk.dma(out_s[kt1 * 128 + off * 64:kt1 * 128 + off * 64 + 64, nc_ * 512:(nc_ + 1) * 512], wpl[wsl][64:128, :], key=f"dqo{wsl}")
            ld_guard[sl] = mk.cnt["v"]
            pi += 1
    mk.barrier()


def emit_L0F(mk, x_in, x_out, sc):
    with contextlib.ExitStack() as ctx:
        emit_ln_pass(mk, ctx, x_in, sc["xnT"], pi_order=True)
    wg_s = mk.scratch("l0f_wg", [2048, 16384], BF16)
    wu_s = mk.scratch("l0f_wu", [2048, 16384], BF16)
    wd_s = mk.scratch("l0f_wd", [16384, 2048], BF16)
    with contextlib.ExitStack() as ctx:
        _dequant_stream(mk, ctx, mk.inp("l0_pg"), mk.inp("l0_sg"), wg_s, 8, 32, 64)
    with contextlib.ExitStack() as ctx:
        _dequant_stream(mk, ctx, mk.inp("l0_pu"), mk.inp("l0_su"), wu_s, 8, 32, 64)
    with contextlib.ExitStack() as ctx:
        _dequant_stream(mk, ctx, mk.inp("l0_pd"), mk.inp("l0_sd"), wd_s, 64, 4, 512, down=True)
    with contextlib.ExitStack() as ctx:
        xch = mk.sb(ctx, "f0_xc", [128, CT * CW], BF16)
        inter = mk.sb(ctx, "f0_in", [128, 64 * 512], BF16)
        acc = mk.sb(ctx, "f0_ac", [128, 4 * 2048], F32)
        wgb = [mk.sb(ctx, f"f0_wg{i}", [128, 2048], BF16) for i in range(2)]
        wub = [mk.sb(ctx, f"f0_wu{i}", [128, 2048], BF16) for i in range(2)]
        wdb = [mk.sb(ctx, f"f0_wd{i}", [128, 4 * 2048], BF16) for i in range(2)]
        g1b = mk.sb(ctx, "f0_g1", [128, 512], F32)
        t2b = mk.sb(ctx, "f0_t2", [128, 512], F32)
        xres = mk.sb(ctx, "f0_rs", [128, 2048], F32)
        ot = mk.sb(ctx, "f0_ot", [128, 2048], F32)
        wg_r = wg_s.rearrange("(kt p) n -> p kt n", p=128)
        wu_r = wu_s.rearrange("(kt p) n -> p kt n", p=128)
        wd_r = wd_s.rearrange("(kt p) o -> p kt o", p=128)
        ps_ga = [0, 0]
        wld_guard = [0, 0]
        wdd_guard = [0, 0]
        ot_guard = None
        xr_guard = 0
        for ch in range(NCH):
            mk.wait("y", "t")
            xt = mk.dma(xch[:, :].rearrange("p (ct w) -> p ct w", w=CW),
                        sc["xnT"].rearrange("(ct p) t -> p ct t", p=128)[:, :, ch * CW:(ch + 1) * CW], key="xchL")
            mk.dwait("t", xt)
            first_acc = True
            il_guard = 0
            for half in range(2):
                for nt in range(64):
                    ntg = half * 64 + nt
                    sl = nt % 2
                    mk.wait("y", "t", wld_guard[sl])
                    tw1 = mk.dma(wgb[sl][:, :].rearrange("p (ct n) -> p ct n", n=128),
                                 wg_r[:, :, ntg * 128:(ntg + 1) * 128], key=f"wgl{sl}")
                    tw2 = mk.dma(wub[sl][:, :].rearrange("p (ct n) -> p ct n", n=128),
                                 wu_r[:, :, ntg * 128:(ntg + 1) * 128], key=f"wul{sl}")
                    mk.dwait("t", tw1)
                    mk.dwait("t", tw2)
                    mk.wait("t", "v", ps_ga[0])

                    def mg(eng, sl=sl):
                        r = None
                        for ct in range(CT):
                            r = eng.matmul(mk.ps_a[:, :], wgb[sl][:, ct * 128:(ct + 1) * 128],
                                           xch[:, ct * CW:(ct + 1) * CW], start=(ct == 0), stop=(ct == CT - 1))
                        return r
                    tga = mk.group("t", mg)
                    mk.wait("t", "v", ps_ga[1])

                    def mu(eng, sl=sl):
                        r = None
                        for ct in range(CT):
                            r = eng.matmul(mk.ps_b[:, :], wub[sl][:, ct * 128:(ct + 1) * 128],
                                           xch[:, ct * CW:(ct + 1) * CW], start=(ct == 0), stop=(ct == CT - 1))
                        return r
                    tgb = mk.group("t", mu)
                    wld_guard[sl] = mk.cnt["t"]
                    mk.wait("v", "t", tga)
                    v1 = mk.op("v", lambda e: e.tensor_copy(out=g1b[:, :], in_=mk.ps_a[:, :]))
                    ps_ga[0] = v1
                    mk.wait("g", "v", v1)
                    mk.wait("g", "s")
                    mk.op("g", lambda e: e.tensor_tensor(out=t2b[:, :], in0=g1b[:, :], in1=g1b[:, :], op=ALU.mult))
                    mk.wait("g", "g")
                    mk.op("g", lambda e: e.tensor_scalar(out=t2b[:, :], in0=t2b[:, :], scalar1=0.7978845608028654 * 0.044715, scalar2=0.7978845608028654, op0=ALU.mult, op1=ALU.add))
                    mk.wait("g", "g")
                    gg = mk.op("g", lambda e: e.tensor_tensor(out=t2b[:, :], in0=t2b[:, :], in1=g1b[:, :], op=ALU.mult))
                    mk.wait("s", "g", gg)
                    s1_ = mk.op("s", lambda e: e.activation(out=t2b[:, :], in_=t2b[:, :], func=AF.Tanh))
                    mk.wait("v", "s", s1_)
                    mk.op("v", lambda e: e.tensor_scalar(out=t2b[:, :], in0=t2b[:, :], scalar1=0.5, scalar2=0.5, op0=ALU.mult, op1=ALU.add))
                    mk.wait("v", "v")
                    mk.op("v", lambda e: e.tensor_tensor(out=t2b[:, :], in0=t2b[:, :], in1=g1b[:, :], op=ALU.mult))
                    mk.wait("v", "v")
                    mk.wait("v", "t", il_guard)
                    mk.wait("v", "t", tgb)
                    iv = mk.op("v", lambda e, nt=nt: e.tensor_tensor(out=inter[:, nt * 512:(nt + 1) * 512], in0=t2b[:, :], in1=mk.ps_b[:, :], op=ALU.mult))
                    ps_ga[1] = iv
                    mk.wait("t", "v", iv)
                mk.wait("t", "v")
                for ktb in range(16):
                    sl = ktb % 2
                    mk.wait("y", "t", wdd_guard[sl])
                    ktg = half * 64 + ktb * 4
                    twd = mk.dma(wdb[sl][:, :].rearrange("p (j o) -> p j o", o=2048),
                                 wd_r[:, ktg:ktg + 4, :], key=f"wdl{sl}")
                    mk.dwait("t", twd)
                    for tt in range(4):
                        for oc in range(4):
                            pi2 = (tt * 4 + oc) % 2
                            psd = mk.ps_a if pi2 == 0 else mk.ps_b
                            mk.wait("t", "v", ps_ga[pi2])

                            def md(eng, sl=sl, ktb=ktb, tt=tt, oc=oc, psd=psd):
                                r = None
                                for j in range(4):
                                    r = eng.matmul(psd[:, :],
                                                   inter[:, (ktb * 4 + j) * 512 + tt * 128:(ktb * 4 + j) * 512 + tt * 128 + 128],
                                                   wdb[sl][:, j * 2048 + oc * 512:j * 2048 + oc * 512 + 512],
                                                   start=(j == 0), stop=(j == 3))
                                return r
                            tgd = mk.group("t", md)
                            mk.wait("v", "t", tgd)
                            if first_acc:
                                av = mk.op("v", lambda e, tt=tt, oc=oc, psd=psd: e.tensor_copy(
                                    out=acc[:, tt * 2048 + oc * 512:tt * 2048 + oc * 512 + 512], in_=psd[:, :]))
                            else:
                                av = mk.op("v", lambda e, tt=tt, oc=oc, psd=psd: e.tensor_tensor(
                                    out=acc[:, tt * 2048 + oc * 512:tt * 2048 + oc * 512 + 512], in0=psd[:, :],
                                    in1=acc[:, tt * 2048 + oc * 512:tt * 2048 + oc * 512 + 512], op=ALU.add))
                            ps_ga[pi2] = av
                    first_acc = False
                    wdd_guard[sl] = mk.cnt["t"]
                il_guard = mk.cnt["t"]
            for tt in range(4):
                ti = ch * 4 + tt
                mk.wait("y", "v", xr_guard)
                rx = mk.dma(xres[:, :], x_in[ti * 128:(ti + 1) * 128, :], key="xr0")
                mk.dwait("v", rx)
                mk.dwait("v", ot_guard)
                mk.wait("v", "v")
                fv = mk.op("v", lambda e, tt=tt: e.tensor_tensor(out=ot[:, :], in0=acc[:, tt * 2048:(tt + 1) * 2048], in1=xres[:, :], op=ALU.add))
                xr_guard = fv
                mk.wait("y", "v", fv)
                ot_guard = mk.dma(x_out[ti * 128:(ti + 1) * 128, :], ot[:, :], key="xo0")
        mk.barrier()


EMITTERS = {}


def _reg():
    EMITTERS["L0A"] = emit_L0A
    EMITTERS["L0F"] = emit_L0F
    EMITTERS["L1A"] = emit_L1A
    EMITTERS["L2A"] = emit_L2A
    EMITTERS["L3A"] = emit_L3A
    EMITTERS["L1F"] = emit_L1F
    EMITTERS["L2F"] = emit_L2F
    EMITTERS["L3F"] = emit_L3F


_reg()


def common_scratch(mk):
    return {
        "xnT": mk.scratch("xnT_s", [D, NTOK], BF16),
        "q": mk.scratch("q_s", [D, NTOK], BF16),
        "k": mk.scratch("k_s", [D, NTOK], BF16),
        "v": mk.scratch("v_s", [NTOK, D], BF16),
        "c": mk.scratch("c_s", [D, NTOK], BF16),
    }


def build_sublayer_program(name, H):
    mk = MK(H, name)
    with contextlib.ExitStack() as gctx:
        mk.gctx = gctx
        x_in = mk.nc.dram_tensor("x", [NTOK, D], F32, kind="ExternalInput").ap()
        mk.inputs["x"] = (None, x_in)
        x_out = mk.out("xo", [NTOK, D], F32)
        sc = common_scratch(mk)
        mk.setup_common(gctx)
        EMITTERS[name](mk, x_in, x_out, sc)
        mk.finish()
    return mk


def build_fused_program(H):
    mk = MK(H, "fused")
    with contextlib.ExitStack() as gctx:
        mk.gctx = gctx
        x_in = mk.nc.dram_tensor("x", [NTOK, D], F32, kind="ExternalInput").ap()
        mk.inputs["x"] = (None, x_in)
        x_out = mk.out("xo", [NTOK, D], F32)
        xs = mk.scratch("x_state", [NTOK, D], F32)
        sc = common_scratch(mk)
        mk.setup_common(gctx)
        names = [n for n in ALL_SUBLAYERS if n in SUBLAYERS]
        for i, n in enumerate(names):
            src = x_in if i == 0 else xs
            dst = x_out if i == len(names) - 1 else xs
            EMITTERS[n](mk, src, dst, sc)
            mk.barrier()
        mk.finish()
    return mk


def kernel(**inputs):
    H = prep_host(inputs)
    x = np.ascontiguousarray(np.asarray(inputs["L0_attn_x"], dtype=np.float32))
    shards = [np.ascontiguousarray(x[2 * c:2 * c + 2].reshape(NTOK, D)) for c in range(8)]
    names = [n for n in ALL_SUBLAYERS if n in SUBLAYERS]
    if FUSED:
        mk = build_fused_program(H)
        res = mk.run([{"x": shards[c]} for c in range(8)])
        outs = [r["xo"] for r in res.results]
    else:
        cur = shards
        for n in names:
            mk = build_sublayer_program(n, H)
            res = mk.run([{"x": cur[c]} for c in range(8)])
            cur = [np.asarray(r["xo"]) for r in res.results]
        outs = cur
    y = np.stack([o.reshape(NBC, SEQ, D) for o in outs], 0).reshape(16, SEQ, D)
    return y.astype(np.float32)
